# version 16
# baseline (speedup 1.0000x reference)
import os
import ml_dtypes
import numpy as np
from contextlib import ExitStack

import concourse.bass as bass
import concourse.mybir as mybir
from concourse.bass_utils import run_bass_kernel_spmd

F32 = mybir.dt.float32
BF16 = mybir.dt.bfloat16
I32 = mybir.dt.int32
AX = mybir.AxisListType.X
OP = mybir.AluOpType
ACT = mybir.ActivationFunctionType

DIM = 2048
SEQ = 2048
HD = 64
EPS = 1e-5

NQ_OF_C = [(8 - cc // 2) * 128 for cc in range(16)]
OFFC = [0]
for _n in NQ_OF_C:
    OFFC.append(OFFC[-1] + _n)


class Prog:
    def __init__(self, sem):
        self.sem = sem
        self.n = 0

    def inc(self, ins, k=1):
        ins.then_inc(self.sem, k)
        self.n += k
        return self.n


class Ctx:
    pass


def build_nc(stage="full"):
    nc = bass.Bass("TRN2", target_bir_lowering=False)
    c = Ctx()
    c.nc = nc
    c.stage = stage
    din = lambda name, shape, dt: nc.dram_tensor(name, list(shape), dt, kind="ExternalInput").ap()

    c.xb = din("xb", [SEQ, DIM], F32)
    c.xown = din("xown", [1024, DIM], F32)
    c.wq_p = din("wq_p", [128, 16, 2048], F32)
    c.wk_p = din("wk_p", [128, 16, 512], F32)
    c.wv_p = din("wv_p", [128, 16, 512], F32)
    c.wo_p = din("wo_p", [128, 16, 2048], F32)
    c.wg_p = din("wg_p", [64, 128, 2048], I32)
    c.wu_p = din("wu_p", [64, 128, 2048], I32)
    c.wd_p = din("wd_p", [4, 16, 128, 2048], I32)
    c.gsc = din("gsc", [128, 64], F32)
    c.usc = din("usc", [128, 64], F32)
    c.dsc = din("dsc", [1, 2048], F32)
    c.qcos = din("qcos", [128, 256], F32)
    c.qsin = din("qsin", [128, 256], F32)
    c.kcos = din("kcos", [128, 512], F32)
    c.ksin = din("ksin", [128, 512], F32)
    c.maskA = din("maskA", [128, 128], BF16)
    c.maskB = din("maskB", [128, 128], BF16)
    c.selmap = din("selmap", [8, 256], F32)
    c.eye = din("eye", [128, 128], F32)

    c.out = nc.dram_tensor("out", [1024, DIM], F32, kind="ExternalOutput").ap()
    c.x2scr = nc.dram_tensor("x2scr", [1024, DIM], F32).ap()

    if stage == "A":
        c.d_k65 = nc.dram_tensor("d_k65", [65, 16384], BF16, kind="ExternalOutput").ap()
        c.d_v65 = nc.dram_tensor("d_v65", [128, 8320], BF16, kind="ExternalOutput").ap()
        c.d_q65 = nc.dram_tensor("d_q65", [65, 32768], BF16, kind="ExternalOutput").ap()
    if stage == "B":
        c.d_att = nc.dram_tensor("d_att", [128, 16384], BF16, kind="ExternalOutput").ap()
    if stage == "C":
        c.d_x2 = nc.dram_tensor("d_x2", [1024, DIM], F32, kind="ExternalOutput").ap()
        c.d_ln3 = nc.dram_tensor("d_ln3", [128, 16384], BF16, kind="ExternalOutput").ap()

    with ExitStack() as top:
        c.top = top
        sem = lambda name: top.enter_context(nc.semaphore(name))
        c.pt = Prog(sem("pt"))
        c.pv = Prog(sem("pv"))
        c.ps = Prog(sem("ps"))
        c.dx = Prog(sem("dx"))
        c.dw = Prog(sem("dw"))
        c.block = top.enter_context(nc.Block())

        sb = lambda name, shape, dt: top.enter_context(nc.sbuf_tensor(name, shape, dt))
        c.eye_b = sb("eye_b", [128, 128], BF16)
        c.eye_f = sb("eye_f", [128, 128], F32)
        c.mA = sb("mA", [128, 128], BF16)
        c.mB = sb("mB", [128, 128], BF16)
        c.ones1f = sb("ones1f", [1, 128], F32)
        c.ones1b = sb("ones1b", [1, 128], BF16)
        c.onescb = sb("onescb", [128, 1], BF16)
        c.stat6 = sb("stat6", [128, 24], F32)
        c.mv = sb("mv", [128, 2], F32)
        c.rstd = sb("rstd", [128, 1], F32)
        c.lntmp = sb("lntmp", [128, 1], F32)
        c.epsv = sb("epsv", [128, 1], F32)
        c.knm = sb("knm", [128, 8], F32)
        c.kn8 = sb("kn8", [128, 8], F32)
        c.kc = sb("kc", [8, 1], F32)
        c.cper = sb("cper", [128, 1], F32)
        c.rowv = sb("rowv", [128, 128], BF16)
        c.rden = sb("rden", [1, 1024], BF16)

        NBELS = 92288
        NFELS = 5632
        c.slb = top.enter_context(nc.sbuf_tensor("slb", [128, NBELS], BF16))
        c.slf = top.enter_context(nc.sbuf_tensor("slf", [128, NFELS], F32))

        def viewb(off, cols, parts=128):
            return c.slb[0:parts, off:off + cols]

        def viewf(off, cols, parts=128):
            return c.slf[0:parts, off:off + cols]

        c.k65 = viewb(0, 16384, parts=65)
        c.v65 = viewb(16384, 8320)
        c.l1 = [viewb(24704 + 2048 * t, 2048) for t in range(12)]
        c.fm = [viewb(49280, 2048), viewb(51328, 2048)]
        c.krot = viewb(53376, 512)
        c.q65 = viewb(24704, 32768, parts=65)
        c.wk_b = viewb(57472, 8192)
        c.wv_b = viewb(65664, 8192)
        c.wqh = viewb(57472, 16384)
        c.att_fm = viewb(57472, 16384)
        c.pairfm = viewb(73856, 16384)
        c.qrot = [viewb(90240, 512), viewb(90752, 512)]
        c.pbuf = [viewb(73856, 9216), viewb(83072, 9216)]
        c.xt = [viewf(0, 2048), viewf(2048, 2048)]
        c.rt1 = viewf(4096, 512)
        c.rt2 = viewf(4608, 512)
        c.kcos_s = viewf(5120, 256)
        c.ksin_s = viewf(5376, 256)
        c.qcos_s = viewf(0, 256)
        c.qsin_s = viewf(256, 256)
        c.qt1 = viewf(4096, 512)
        c.qt2 = viewf(4608, 512)
        c.qn2 = viewf(2048, 256)
        c.selm = viewf(2304, 256, parts=8)
        c.sqt = viewf(4096, 128)
        c.sq2 = viewf(4224, 128)
        c.dens = viewf(0, 1024, parts=1)
        c.lntmp2 = viewf(1024, 1024, parts=1)
        c.rbs2 = [c.slf[0:64, 2560:3584], c.slf[64:128, 2560:3584]]
        c.ln3_fm = viewb(0, 16384)
        c.woh = viewb(16384, 16384)
        c.woh2 = viewb(73856, 16384)
        c.aptok = viewb(32768, 16384)
        c.l3 = [viewb(49152, 2048), viewb(51200, 2048)]
        c.apf = [viewf(4096, 512), viewf(4608, 512)]
        c.xo = [viewf(0, 2048)] * 2
        c.x2t = [viewf(2048, 2048)] * 2
        c.gsc_s = viewf(5120, 64)
        c.usc_s = viewf(5184, 64)
        c.hdn = viewb(16384, 65536)
        c.wgt = [viewb(81920, 2048), viewb(83968, 2048)]
        c.wut = [viewb(86016, 2048), viewb(88064, 2048)]
        c.wdt = [viewb(81920, 2048), viewb(83968, 2048), viewb(86016, 2048)]
        c.gs = viewb(90112, 1024)
        c.ubf = viewb(91136, 1024)
        c.us = c.ubf
        c.y = viewb(0, 16384)
        c.gtmp = viewf(0, 1024)
        c.cs3 = viewf(2048, 1024)
        c.s3 = viewf(3072, 1024, parts=1)
        c.sdn = viewf(3072, 1024, parts=1)
        c.m128f = viewf(2048, 512, parts=1)
        c.dsc_s = viewf(0, 2048, parts=1)
        c.dsc_bc = viewf(0, 2048)
        c.x2r = [viewf(0, 2048), viewf(2048, 2048)]
        c.ln4b = viewf(4096, 1024)

        c.psA = top.enter_context(nc.psum_tensor("psA", [128, 2048], F32))
        c.psB = top.enter_context(nc.psum_tensor("psB", [128, 2048], F32))

        def bank(i, parts=128):
            t = c.psA if i < 4 else c.psB
            return t[0:parts, (i % 4) * 512:(i % 4) * 512 + 512]

        c.bank = bank

        emit_prelude(c)
        emit_phase_a(c)
        emit_phase_a2(c)
        if stage == "A":
            emit_dump_a(c)
        else:
            emit_phase_b(c)
            if stage == "B":
                emit_dump_b(c)
            else:
                emit_phase_c(c)
                if stage == "C":
                    emit_dump_c(c)
                else:
                    emit_phase_d(c)
                    emit_phase_e(c)
    return nc



def _preload_set(c, e, funcs):
    from concourse.hw_specs import get_activation_tables
    for idx, fns in enumerate(get_activation_tables(c.nc.m.arch).values()):
        if all(f in fns for f in funcs):
            return e.add_instruction(mybir.InstLoadActFuncSet(
                name=c.nc.get_next_instruction_name(),
                act_func_set_id=idx, ins=[], outs=[]))
    raise ValueError(f"no activation set contains {funcs}")


def _ln_stats_edge(c, e, src_ap, n_free):
    ngrp = (n_free + 511) // 512
    for g in range(ngrp):
        ins = e.bn_stats(out=c.stat6[:, g * 6:(g + 1) * 6],
                         in_=src_ap[:, g * 512:min((g + 1) * 512, n_free)])
    c.pv.inc(ins)
    e.wait_ge(c.pv.sem, c.pv.n)
    return e.bn_aggr(out=c.mv[:], in_=c.stat6[:, 0:6 * ngrp])


def _rstd_emit(c, e):
    c.ps.inc(e.activation(out=c.lntmp[:], in_=c.mv[:, 1:2], func=ACT.Ln,
                          bias=c.epsv[:, 0:1], scale=1.0))
    e.wait_ge(c.ps.sem, c.ps.n)
    return e.activation(out=c.rstd[:], in_=c.lntmp[:], func=ACT.Exp,
                        bias=0.0, scale=-0.5)


def _rope(c, e, psum_src, nheads, cos_ap, sin_ap, dest_ap, t1, t2):
    src = psum_src.rearrange("p (h j two) -> p h j two", h=nheads, j=32, two=2)
    ev = src[:, :, :, 0]
    ov = src[:, :, :, 1]
    cb = cos_ap[:, None, :].broadcast_to((128, nheads, 32))
    sbb = sin_ap[:, None, :].broadcast_to((128, nheads, 32))
    t1v = t1[:, 0:nheads * 32].rearrange("p (h j) -> p h j", h=nheads)
    t2v = t2[:, 0:nheads * 32].rearrange("p (h j) -> p h j", h=nheads)
    dv = dest_ap.rearrange("p (h half j) -> p h half j", h=nheads, half=2, j=32)
    e.tensor_tensor(out=t1v, in0=ev, in1=cb, op=OP.mult)
    c.pv.inc(e.tensor_tensor(out=t2v, in0=ov, in1=sbb, op=OP.mult))
    e.wait_ge(c.pv.sem, c.pv.n)
    c.pv.inc(e.tensor_sub(dv[:, :, 0, :], t1v, t2v))
    e.wait_ge(c.pv.sem, c.pv.n)
    e.tensor_tensor(out=t1v, in0=ev, in1=sbb, op=OP.mult)
    c.pv.inc(e.tensor_tensor(out=t2v, in0=ov, in1=cb, op=OP.mult))
    e.wait_ge(c.pv.sem, c.pv.n)
    return e.tensor_add(dv[:, :, 1, :], t1v, t2v)


def emit_prelude(c):
    block = c.block

    @block.sync
    def _(e):
        for dst, srcap in [(c.eye_f, c.eye), (c.mA, c.maskA), (c.mB, c.maskB),
                           (c.kcos_s, c.kcos[:, 0:256]),
                           (c.ksin_s, c.ksin[:, 0:256])]:
            c.dx.inc(e.dma_start(out=dst[:], in_=srcap), 16)
        e.wait_ge(c.dx.sem, c.dx.n)

    @block.gpsimd
    def _(e):
        c.dw.inc(e.dma_start(out=c.eye_b[:], in_=c.eye), 16)

    @block.vector
    def _(e):
        e.memset(c.ones1f[:], 1.0)
        e.memset(c.ones1b[:], 1.0)
        e.memset(c.epsv[:], EPS)
        c.pv.inc(e.memset(c.onescb[:], 1.0))
    c.v_pre = c.pv.n

    @block.scalar
    def _(e):
        _preload_set(c, e, [ACT.Exp, ACT.Ln])


def emit_phase_a(c):
    nc, block = c.nc, c.block
    xt, l1, fm, krot, rt1, rt2 = c.xt, c.l1, c.fm, c.krot, c.rt1, c.rt2

    @block.vector
    def _(e):
        e.memset(c.k65[64:65, :], 1.0)
        v = c.v65[:].rearrange("p (t h s) -> p t h s", t=16, h=8, s=65)
        c.pv.inc(e.memset(v[:, :, :, 64:65], 1.0))

    @block.gpsimd
    def _(e):
        e.wait_ge(c.dw.sem, 16)
        c.dw.inc(e.dma_start(out=c.wk_b[:].rearrange("p (a b) -> p a b", a=16),
                             in_=c.wk_p), 16)
        e.wait_ge(c.dw.sem, 32)
        c.dw.inc(e.dma_start(out=c.wv_b[:].rearrange("p (a b) -> p a b", a=16),
                             in_=c.wv_p), 16)
    c.dw_kv = c.dw.n

    v_ln1 = {}
    v_fmev = {}
    v_krope = {}
    v_vev = {}
    v_ktev = {}
    t_tr = {}
    t_kv = {}
    t_kt = {}
    s_rstd = {}
    dx_x = {}

    def emit_a0(t):
        @block.sync
        def _(e, t=t):
            if t >= 2:
                e.wait_ge(c.pv.sem, v_ln1[t - 2])
            if t >= 1:
                e.wait_ge(c.dx.sem, c.dx.n)
            c.dx.inc(e.dma_start(out=xt[t % 2][:], in_=c.xb[t * 128:(t + 1) * 128, :]), 16)
            dx_x[t] = c.dx.n

        @block.vector
        def _(e, t=t):
            if t == 0:
                e.wait_ge(c.pv.sem, c.v_pre)
            e.wait_ge(c.dx.sem, dx_x[t])
            c.pv.inc(_ln_stats_edge(c, e, xt[t % 2][:], 2048))

        @block.scalar
        def _(e, t=t):
            e.wait_ge(c.pv.sem, c.pv.n)
            s_rstd[t] = c.ps.inc(_rstd_emit(c, e))

        @block.vector
        def _(e, t=t):
            e.wait_ge(c.ps.sem, s_rstd[t])
            if t >= 12:
                e.wait_ge(c.pt.sem, t_tr[t - 12])
            ins = e.tensor_scalar(out=l1[t % 12][:], in0=xt[t % 2][:],
                                  scalar1=c.mv[:, 0:1], scalar2=c.rstd[:, 0:1],
                                  op0=OP.subtract, op1=OP.mult)
            v_ln1[t] = c.pv.inc(ins)

    def emit_a1(t):
        @block.tensor
        def _(e, t=t):
            e.wait_ge(c.pv.sem, v_ln1[t])
            if t >= 1:
                e.wait_ge(c.ps.sem, v_fmev[t - 1])
            e.wait_ge(c.dw.sem, c.dw_kv)
            for dm in range(16):
                ins = e.matmul(c.bank(dm // 4)[:, (dm % 4) * 128:(dm % 4 + 1) * 128],
                               l1[t % 12][:, dm * 128:(dm + 1) * 128],
                               c.eye_b[:, :], start=True, stop=True)
            t_tr[t] = c.pt.inc(ins)

        @block.scalar
        def _(e, t=t):
            e.wait_ge(c.pt.sem, t_tr[t])
            for q in range(4):
                e.copy(out=fm[t % 2][:, q * 512:(q + 1) * 512], in_=c.bank(q))
            pfv = c.pairfm[:].rearrange("p (d o) -> p d o", d=16)
            for bg in range(4):
                sl = c.bank(bg).rearrange("p (q cc) -> p q cc", q=4)
                ins = e.copy(out=pfv[:, bg * 4:(bg + 1) * 4, t * 64:t * 64 + 64],
                             in_=sl[:, :, 0:64])
            v_fmev[t] = c.ps.inc(ins)

        @block.tensor
        def _(e, t=t):
            e.wait_ge(c.ps.sem, v_fmev[t])
            if t >= 1:
                e.wait_ge(c.pv.sem, v_krope[t - 1])
                e.wait_ge(c.pv.sem, v_vev[t - 1])
            for dm in range(16):
                e.matmul(c.bank(4), fm[t % 2][:, dm * 128:(dm + 1) * 128],
                         c.wk_b[:, dm * 512:(dm + 1) * 512],
                         start=(dm == 0), stop=(dm == 15))
                ins = e.matmul(c.bank(5), fm[t % 2][:, dm * 128:(dm + 1) * 128],
                               c.wv_b[:, dm * 512:(dm + 1) * 512],
                               start=(dm == 0), stop=(dm == 15))
            t_kv[t] = c.pt.inc(ins)

        @block.vector
        def _(e, t=t):
            e.wait_ge(c.pt.sem, t_kv[t])
            if t == 8:
                e.wait_ge(c.dx.sem, c.dx_kh)
            tm = t % 8
            c.pv.inc(_rope(c, e, c.bank(4), 8, c.kcos_s[:, tm * 32:(tm + 1) * 32],
                           c.ksin_s[:, tm * 32:(tm + 1) * 32], krot[:], rt1, rt2))
            e.wait_ge(c.pv.sem, c.pv.n)
            c.pv.inc(e.tensor_mul(rt1[:], krot[:], krot[:]))
            e.wait_ge(c.pv.sem, c.pv.n)
            if t == 0:
                ins = e.reduce_sum(out=c.knm[:, :],
                                   in_=rt1[:].rearrange("p (h d) -> p h d", h=8), axis=AX)
            else:
                ins = e.reduce_sum(out=c.kn8[:, :],
                                   in_=rt1[:].rearrange("p (h d) -> p h d", h=8), axis=AX)
                c.pv.inc(ins)
                e.wait_ge(c.pv.sem, c.pv.n)
                ins = e.tensor_max(c.knm[:, :], c.knm[:, :], c.kn8[:, :])
            v_krope[t] = c.pv.inc(ins)
            vv = c.v65[:].rearrange("p (tt h s) -> p tt h s", tt=16, h=8, s=65)
            ins = e.tensor_copy(out=vv[:, t, :, 0:64],
                                in_=c.bank(5).rearrange("p (h d) -> p h d", h=8))
            v_vev[t] = c.pv.inc(ins)

        @block.tensor
        def _(e, t=t):
            e.wait_ge(c.pv.sem, v_krope[t])
            if t >= 1:
                e.wait_ge(c.ps.sem, v_ktev[t - 1])
            for hp in range(4):
                ins = e.matmul(c.bank(6 + hp // 2)[:, (hp % 2) * 128:(hp % 2 + 1) * 128],
                               krot[:, hp * 128:(hp + 1) * 128], c.eye_b[:, :],
                               start=True, stop=True)
            t_kt[t] = c.pt.inc(ins)

        @block.scalar
        def _(e, t=t):
            e.wait_ge(c.pt.sem, t_kt[t])
            for hp in range(4):
                bk = c.bank(6 + hp // 2)[:, (hp % 2) * 128:(hp % 2 + 1) * 128]
                e.copy(
                    out=c.k65[0:64, (2 * hp) * 2048 + t * 128:(2 * hp) * 2048 + (t + 1) * 128],
                    in_=bk[0:64, :])
                ins = e.copy(
                    out=c.k65[0:64, (2 * hp + 1) * 2048 + t * 128:(2 * hp + 1) * 2048 + (t + 1) * 128],
                    in_=bk[64:128, :])
            v_ktev[t] = c.ps.inc(ins)

        if t == 7:
            @block.sync
            def _(e):
                e.wait_ge(c.pv.sem, v_krope[7])
                e.wait_ge(c.dx.sem, c.dx.n)
                c.dx.inc(e.dma_start(out=c.kcos_s[:], in_=c.kcos[:, 256:512]), 16)
                c.dx.inc(e.dma_start(out=c.ksin_s[:], in_=c.ksin[:, 256:512]), 16)
                c.dx_kh = c.dx.n

    for t in range(18):
        if t < 16:
            emit_a0(t)
        if t >= 2:
            emit_a1(t - 2)

    c.v_fmev_last = v_fmev[15]
    c.s_a_last = c.ps.n


def emit_phase_a2(c):
    nc, block = c.nc, c.block
    bar_pt, bar_pv = c.pt.n, c.pv.n
    wqh, qrot, qt1, qt2 = c.wqh, c.qrot, c.qt1, c.qt2

    t_q = {}
    v_qr = {}
    t_qt = {}
    v_qte = {}
    dwq = {}

    for h in range(2):
        @block.gpsimd
        def _(e, h=h):
            if h == 0:
                e.wait_ge(c.pt.sem, bar_pt)
                e.wait_ge(c.pv.sem, bar_pv)
            if h == 1:
                e.wait_ge(c.pt.sem, t_q[(0, 7, 1)])
            e.wait_ge(c.dw.sem, c.dw.n)
            if h == 0:
                c.dw.inc(e.dma_start(out=c.qcos_s[:], in_=c.qcos), 16)
                c.dw.inc(e.dma_start(out=c.qsin_s[:], in_=c.qsin), 16)
                c.dw.inc(e.dma_start(out=c.selm[:], in_=c.selmap), 16)
                e.wait_ge(c.dw.sem, c.dw.n)
            ins = e.dma_start(out=wqh[:].rearrange("p (a b) -> p a b", a=16),
                              in_=c.wq_p[:, :, h * 1024:(h + 1) * 1024])
            dwq[h] = c.dw.inc(ins, 16)

        for p in range(8):
            for ob in range(2):
                key_prev = (h, p - 1, ob) if p >= 1 else ((0, 7, ob) if h == 1 else None)

                @block.tensor
                def _(e, h=h, p=p, ob=ob, key_prev=key_prev):
                    e.wait_ge(c.dw.sem, dwq[h])
                    if key_prev is None:
                        e.wait_ge(c.ps.sem, c.v_fmev_last)
                    else:
                        e.wait_ge(c.pv.sem, v_qr[key_prev])
                    for dm in range(16):
                        ins = e.matmul(c.bank(ob),
                                       c.pairfm[:, dm * 1024 + p * 128:dm * 1024 + (p + 1) * 128],
                                       wqh[:, dm * 1024 + ob * 512:dm * 1024 + (ob + 1) * 512],
                                       start=(dm == 0), stop=(dm == 15))
                    t_q[(h, p, ob)] = c.pt.inc(ins)

                @block.vector
                def _(e, h=h, p=p, ob=ob):
                    e.wait_ge(c.pt.sem, t_q[(h, p, ob)])
                    c.pv.inc(_rope(c, e, c.bank(ob), 8,
                                   c.qcos_s[:, p * 32:(p + 1) * 32],
                                   c.qsin_s[:, p * 32:(p + 1) * 32], qrot[ob][:], qt1, qt2))
                    e.wait_ge(c.pv.sem, c.pv.n)
                    c.pv.inc(e.tensor_mul(qt1[:], qrot[ob][:], qrot[ob][:]))
                    e.wait_ge(c.pv.sem, c.pv.n)
                    s0 = (h * 16 + ob * 8) * 8 + p
                    ins = e.reduce_sum(out=c.qn2[:, s0:s0 + 57:8],
                                       in_=qt1[:].rearrange("p (hh d) -> p hh d", hh=8),
                                       axis=AX)
                    v_qr[(h, p, ob)] = c.pv.inc(ins)

                @block.tensor
                def _(e, h=h, p=p, ob=ob, key_prev=key_prev):
                    e.wait_ge(c.pv.sem, v_qr[(h, p, ob)])
                    if key_prev is not None:
                        e.wait_ge(c.ps.sem, v_qte[key_prev])
                    else:
                        e.wait_ge(c.ps.sem, c.s_a_last)
                    for hp in range(4):
                        ins = e.matmul(
                            c.bank(4 + 2 * ob + hp // 2)[:, (hp % 2) * 128:(hp % 2 + 1) * 128],
                            qrot[ob][:, hp * 128:(hp + 1) * 128], c.eye_b[:, :],
                            start=True, stop=True)
                    t_qt[(h, p, ob)] = c.pt.inc(ins)

                @block.scalar
                def _(e, h=h, p=p, ob=ob):
                    e.wait_ge(c.pt.sem, t_qt[(h, p, ob)])
                    base_head = h * 16 + ob * 8
                    for hp in range(4):
                        bk = c.bank(4 + 2 * ob + hp // 2)[:, (hp % 2) * 128:(hp % 2 + 1) * 128]
                        q0 = base_head + 2 * hp
                        e.copy(
                            out=c.q65[0:64, q0 * 1024 + p * 128:q0 * 1024 + (p + 1) * 128],
                            in_=bk[0:64, :])
                        ins = e.copy(
                            out=c.q65[0:64, (q0 + 1) * 1024 + p * 128:(q0 + 1) * 1024 + (p + 1) * 128],
                            in_=bk[64:128, :])
                    v_qte[(h, p, ob)] = c.ps.inc(ins)

    c.v_a2_last = c.pv.n
    c.s_a2_last = c.ps.n
    c.t_a2_last = c.pt.n


def emit_dump_a(c):
    nc, block = c.nc, c.block

    @block.sync
    def _(e):
        e.wait_ge(c.pv.sem, c.v_a2_last)
        e.wait_ge(c.ps.sem, c.s_a2_last)
        c.dx.inc(e.dma_start(out=c.d_k65, in_=c.k65[:, :]), 16)
        c.dx.inc(e.dma_start(out=c.d_v65, in_=c.v65[:, :]), 16)
        c.dx.inc(e.dma_start(out=c.d_q65, in_=c.q65[:, :]), 16)
        e.wait_ge(c.dx.sem, c.dx.n)


def emit_phase_b(c):
    nc, block = c.nc, c.block
    pbuf = c.pbuf

    s2 = [c.psA[:, 0:1024], c.psA[:, 1024:2048], c.psB[:, 0:1024]]
    pvb1 = [c.bank(6, parts=65), c.bank(7, parts=65)]

    stab = {}
    stab_lvl = int(os.environ.get("ATH_STAB", "6"))
    if stab_lvl == 0:
        c.v_att_all = c.pv.n
        return

    @block.tensor
    def _(e):
        e.wait_ge(c.pv.sem, c.v_a2_last)
        e.wait_ge(c.ps.sem, c.s_a2_last)
        ins = e.matmul(c.psA[0:8, 0:128], c.knm[:, :], c.eye_f[:, :],
                       start=True, stop=True)
        stab["t_kt"] = c.pt.inc(ins)

    @block.vector
    def _(e):
        e.wait_ge(c.pt.sem, stab["t_kt"])
        stab["v_kc2"] = c.pv.inc(e.reduce_max(out=c.kc[:, 0:1], in_=c.psA[0:8, 0:128], axis=AX))

    @block.scalar
    def _(e):
        e.wait_ge(c.pv.sem, stab["v_kc2"])
        c.ps.inc(e.activation(out=c.lntmp[0:8, 0:1], in_=c.kc[:, 0:1], func=ACT.Ln,
                               bias=0.0, scale=1.0))
        e.wait_ge(c.ps.sem, c.ps.n)
        ins = e.activation(out=c.kc[:, 0:1], in_=c.lntmp[0:8, 0:1], func=ACT.Exp,
                           bias=0.0, scale=0.5)
        stab["s_kc"] = c.ps.inc(ins)

    if stab_lvl == 1:
        c.v_att_all = c.pv.n
        return
    dx_row = {}
    for ti in range(2, 2 + 99):
        break
    for ti in range(2 if stab_lvl >= 5 else 1):
        @block.tensor
        def _(e, ti=ti):
            e.wait_ge(c.ps.sem, stab["s_kc"])
            if ti == 1:
                e.wait_ge(c.pv.sem, stab[("v_rowv", 0)])
            e.matmul(c.psA[0:128, 512:513], c.selm[0:8, ti * 128:(ti + 1) * 128],
                     c.kc[0:8, 0:1], start=True, stop=True)
            ins = e.matmul(c.psA[:, 1024:1152], c.qn2[:, ti * 128:(ti + 1) * 128],
                           c.eye_f[:, :], start=True, stop=True)
            stab[("t_q", ti)] = c.pt.inc(ins)

        if stab_lvl == 2:
            continue
        sub = os.environ.get("ATH_STAB3", "abc")

        if "a" in sub:
            @block.vector
            def _(e, ti=ti):
                e.wait_ge(c.pt.sem, stab[("t_q", ti)])
                ins = e.tensor_copy(out=c.cper[:, 0:1], in_=c.psA[0:128, 512:513])
                stab[("v_cper", ti)] = c.pv.inc(ins)

        if "b" not in sub:
            continue

        @block.scalar
        def _(e, ti=ti):
            e.wait_ge(c.pt.sem, stab[("t_q", ti)])
            c.ps.inc(e.copy(out=c.sq2[:, :], in_=c.psA[:, 1024:1152]))
            e.wait_ge(c.ps.sem, c.ps.n)
            c.ps.inc(e.activation(out=c.sqt[:, :], in_=c.sq2[:, :], func=ACT.Ln,
                                  bias=0.0, scale=1.0))
            e.wait_ge(c.ps.sem, c.ps.n)
            ins = e.activation(out=c.sq2[:, :], in_=c.sqt[:, :], func=ACT.Exp,
                               bias=0.0, scale=0.5)
            stab[("s_sq", ti)] = c.ps.inc(ins)

        if "c" not in sub:
            continue

        @block.vector
        def _(e, ti=ti):
            e.wait_ge(c.ps.sem, stab[("s_sq", ti)])
            e.wait_ge(c.pv.sem, stab[("v_cper", ti)])
            if ti == 1:
                e.wait_ge(c.dx.sem, dx_row[0])
            ins = e.tensor_scalar(out=c.rowv[:, :], in0=c.sq2[:, :],
                                  scalar1=c.cper[:, 0:1], scalar2=-1.0,
                                  op0=OP.mult, op1=OP.mult)
            stab[("v_rowv", ti)] = c.pv.inc(ins)

        if stab_lvl == 3:
            continue

        @block.sync
        def _(e, ti=ti):
            e.wait_ge(c.pv.sem, stab[("v_rowv", ti)])
            e.wait_ge(c.dx.sem, c.dx.n)
            c.dx.inc(e.dma_start(out=c.q65[64:65, ti * 16384:(ti + 1) * 16384],
                                 in_=c.rowv[:, :]), 16)
            dx_row[ti] = c.dx.n

    if stab_lvl < 6:
        c.v_att_all = c.pv.n
        return
    if stab_lvl < 6:
        c.v_att_all = c.pv.n
        return
    t_s2 = {}
    s_exp = {}
    v_pm = {}
    t_pv = {}
    v_raw = {}
    v_dens = {}
    s_rden = {}
    t_bc = {}
    v_rbs = {}
    v_att = {}
    PREV_CC = [15, 13, 14]

    nheads_b = int(os.environ.get("ATH_BHEADS", "32"))

    def emit_tail(qh):
        @block.tensor
        def _(e):
            e.wait_ge(c.ps.sem, s_rden[qh])
            if (qh + 1, 3) in s_exp:
                e.wait_ge(c.ps.sem, s_exp[(qh + 1, 3)])
            e.matmul(s2[0][0:64, 0:512], c.ones1b[0:1, 0:64], c.rden[0:1, 0:512],
                     start=True, stop=True)
            ins = e.matmul(s2[0][0:64, 512:1024], c.ones1b[0:1, 0:64],
                           c.rden[0:1, 512:1024], start=True, stop=True)
            t_bc[qh] = c.pt.inc(ins)

        @block.vector
        def _(e):
            e.wait_ge(c.pt.sem, t_bc[qh])
            if (qh - 2) in v_att:
                e.wait_ge(c.pv.sem, v_att[qh - 2])
            rbsv = c.rbs2[qh % 2]
            ins = e.tensor_copy(out=rbsv, in_=s2[0][0:64, 0:1024])
            v_rbs[qh] = c.pv.inc(ins)
            e.wait_ge(c.pv.sem, v_rbs[qh])
            e.wait_ge(c.pv.sem, v_raw[qh])
            j = qh // 2
            r0 = (qh % 2) * 64
            dst = c.att_fm[r0:r0 + 64, j * 1024:(j + 1) * 1024]
            ins = e.tensor_tensor(out=dst[:, :], in0=dst[:, :],
                                  in1=rbsv, op=OP.mult)
            v_att[qh] = c.pv.inc(ins)

    for qh in range(nheads_b):
        kv = qh // 4
        pvb = pvb1

        def emit_pv(cc, qh=qh, kv=kv, pvb=pvb):
            p0 = cc // 2
            N = NQ_OF_C[cc]

            @block.tensor
            def _(e, qh=qh, cc=cc, p0=p0, N=N, kv=kv, pvb=pvb):
                e.wait_ge(c.pv.sem, v_pm[(qh, cc)])
                if cc == 0 and (qh - 1) in v_raw:
                    e.wait_ge(c.pv.sem, v_raw[qh - 1])
                lhs = c.v65[:, cc * 520 + kv * 65:cc * 520 + (kv + 1) * 65]
                pbf = pbuf[qh % 2]
                if p0 < 4:
                    wA = (4 - p0) * 128
                    e.matmul(pvb[0][:, p0 * 128:512], lhs,
                             pbf[:, OFFC[cc]:OFFC[cc] + wA],
                             start=(cc == 0), stop=(cc == 7), skip_group_check=True)
                    ins = e.matmul(pvb[1][:, 0:512], lhs,
                                   pbf[:, OFFC[cc] + wA:OFFC[cc] + N],
                                   start=(cc == 0), stop=(cc == 15), skip_group_check=True)
                else:
                    ins = e.matmul(pvb[1][:, (p0 - 4) * 128:512], lhs,
                                   pbf[:, OFFC[cc]:OFFC[cc] + N],
                                   start=False, stop=(cc == 15), skip_group_check=True)
                if cc == 15:
                    t_pv[qh] = c.pt.inc(ins)

        for cc in range(16):
            p0 = cc // 2
            N = NQ_OF_C[cc]
            si = cc % 3
            if cc == 4 and qh >= 1:
                emit_tail(qh - 1)

            @block.tensor
            def _(e, qh=qh, cc=cc, p0=p0, N=N, si=si, kv=kv):
                if qh == 0 and cc == 0:
                    e.wait_ge(c.dx.sem, dx_row[1])
                key = (qh, cc - 3)
                if key in s_exp:
                    e.wait_ge(c.ps.sem, s_exp[key])
                elif (qh - 1, PREV_CC[si]) in s_exp:
                    e.wait_ge(c.ps.sem, s_exp[(qh - 1, PREV_CC[si])])
                if cc == 6 and (qh - 1) in v_rbs:
                    e.wait_ge(c.pv.sem, v_rbs[qh - 1])
                if qh == 0 and cc < 3:
                    e.wait_ge(c.pv.sem, stab[("v_rowv", 1)])
                for half in range(2):
                    n = min(512, N - half * 512)
                    if n <= 0:
                        break
                    ins = e.matmul(s2[si][:, half * 512:half * 512 + n],
                                   c.k65[:, kv * 2048 + cc * 128:kv * 2048 + (cc + 1) * 128],
                                   c.q65[:, qh * 1024 + p0 * 128 + half * 512:
                                         qh * 1024 + p0 * 128 + half * 512 + n],
                                   start=True, stop=True)
                t_s2[(qh, cc)] = c.pt.inc(ins)

            @block.scalar
            def _(e, qh=qh, cc=cc, N=N, si=si):
                e.wait_ge(c.pt.sem, t_s2[(qh, cc)])
                if cc == 0 and (qh - 2) in t_pv:
                    e.wait_ge(c.pt.sem, t_pv[qh - 2])
                ins = e.activation(out=pbuf[qh % 2][:, OFFC[cc]:OFFC[cc] + N],
                                   in_=s2[si][:, 0:N], func=ACT.Exp, bias=0.0, scale=1.0)
                s_exp[(qh, cc)] = c.ps.inc(ins)

            @block.vector
            def _(e, qh=qh, cc=cc, si=si):
                e.wait_ge(c.ps.sem, s_exp[(qh, cc)])
                m = c.mA if cc % 2 == 0 else c.mB
                sl = pbuf[qh % 2][:, OFFC[cc]:OFFC[cc] + 128]
                ins = e.tensor_tensor(out=sl, in0=sl, in1=m[:, :], op=OP.mult)
                v_pm[(qh, cc)] = c.pv.inc(ins)

            if cc >= 2:
                emit_pv(cc - 2)
        emit_pv(14)
        emit_pv(15)

        @block.vector
        def _(e, qh=qh, pvb=pvb):
            e.wait_ge(c.pt.sem, t_pv[qh])
            if (qh - 1) in t_bc:
                e.wait_ge(c.pt.sem, t_bc[qh - 1])
            e.tensor_copy(out=c.dens[0:1, 0:512], in_=pvb[0][64:65, 0:512])
            ins = e.tensor_copy(out=c.dens[0:1, 512:1024], in_=pvb[1][64:65, 0:512])
            v_dens[qh] = c.pv.inc(ins)
            j = qh // 2
            r0 = (qh % 2) * 64
            dst = c.att_fm[r0:r0 + 64, j * 1024:(j + 1) * 1024]
            e.tensor_copy(out=dst[:, 0:512], in_=pvb[0][0:64, 0:512])
            ins = e.tensor_copy(out=dst[:, 512:1024], in_=pvb[1][0:64, 0:512])
            v_raw[qh] = c.pv.inc(ins)

        @block.scalar
        def _(e, qh=qh):
            e.wait_ge(c.pv.sem, v_dens[qh])
            c.ps.inc(e.activation(out=c.lntmp2[0:1, :], in_=c.dens[0:1, :], func=ACT.Ln,
                                  bias=0.0, scale=1.0))
            e.wait_ge(c.ps.sem, c.ps.n)
            ins = e.activation(out=c.rden[0:1, :], in_=c.lntmp2[0:1, :], func=ACT.Exp,
                               bias=0.0, scale=-1.0)
            s_rden[qh] = c.ps.inc(ins)

    emit_tail(nheads_b - 1)
    c.v_att_all = v_att[nheads_b - 1] if nheads_b else c.pv.n
    c.t_pv_last = t_pv[nheads_b - 1] if nheads_b else c.pt.n


def emit_dump_b(c):
    nc, block = c.nc, c.block

    @block.sync
    def _(e):
        e.wait_ge(c.pv.sem, c.v_att_all)
        c.dx.inc(e.dma_start(out=c.d_att, in_=c.att_fm[:, :]), 16)
        e.wait_ge(c.dx.sem, c.dx.n)


def emit_phase_c(c):
    nc, block = c.nc, c.block
    woh_buf = c.woh
    apf, aptok, xo, x2t, l3 = c.apf, c.aptok, c.xo, c.x2t, c.l3

    dwo = {}
    t_wo = {}
    v_apf = {}
    t_tr = {}
    v_apt = {}

    bar_pt = c.pt.n
    bar_ps = c.ps.n

    @block.gpsimd
    def _(e):
        e.wait_ge(c.pt.sem, c.t_pv_last)
        e.wait_ge(c.dw.sem, c.dw.n)
        ins = e.dma_start(out=woh_buf[:].rearrange("p (a b) -> p a b", a=16),
                          in_=c.wo_p[:, :, 0:1024])
        dwo[0] = c.dw.inc(ins, 16)
        e.wait_ge(c.dw.sem, dwo[0])
        ins = e.dma_start(out=c.woh2[:].rearrange("p (a b) -> p a b", a=16),
                          in_=c.wo_p[:, :, 1024:2048])
        dwo[1] = c.dw.inc(ins, 16)
        c.dx.inc(e.dma_start(out=c.gsc_s[:], in_=c.gsc), 16)
        c.dx.inc(e.dma_start(out=c.usc_s[:], in_=c.usc), 16)
        c.dx_sc = c.dx.n

    for h in range(2):
        for grp in range(2):
            for do in range(8):
                idx = (h * 2 + grp) * 8 + do

                @block.tensor
                def _(e, h=h, grp=grp, do=do, idx=idx):
                    if idx == 0:
                        e.wait_ge(c.dw.sem, dwo[0])
                        e.wait_ge(c.pv.sem, c.v_att_all)
                    if (h, grp, do) == (1, 0, 0):
                        e.wait_ge(c.dw.sem, dwo[1])
                    if idx - 2 >= 0:
                        e.wait_ge(c.pv.sem, v_apf[idx - 2])
                    wsrc = woh_buf if h == 0 else c.woh2
                    for dm in range(16):
                        ins = e.matmul(
                            c.bank(idx % 2),
                            wsrc[:, dm * 1024 + do * 128:dm * 1024 + (do + 1) * 128],
                            c.att_fm[:, dm * 1024 + grp * 512:dm * 1024 + (grp + 1) * 512],
                            start=(dm == 0), stop=(dm == 15))
                    t_wo[idx] = c.pt.inc(ins)

                @block.vector
                def _(e, idx=idx):
                    e.wait_ge(c.pt.sem, t_wo[idx])
                    if idx - 2 >= 0:
                        e.wait_ge(c.pt.sem, t_tr[idx - 2])
                    ins = e.tensor_copy(out=apf[idx % 2][:], in_=c.bank(idx % 2))
                    v_apf[idx] = c.pv.inc(ins)

                @block.tensor
                def _(e, idx=idx):
                    e.wait_ge(c.pv.sem, v_apf[idx])
                    if idx - 2 >= 0:
                        e.wait_ge(c.ps.sem, v_apt[idx - 2])
                    for q in range(4):
                        ins = e.matmul(c.bank(2 + idx % 2)[:, q * 128:(q + 1) * 128],
                                       apf[idx % 2][:, q * 128:(q + 1) * 128],
                                       c.eye_f[:, :], start=True, stop=True)
                    t_tr[idx] = c.pt.inc(ins)

                @block.scalar
                def _(e, h=h, grp=grp, do=do, idx=idx):
                    e.wait_ge(c.pt.sem, t_tr[idx])
                    av = aptok[:].rearrange("p (t d) -> p t d", t=8)
                    fcol = h * 1024 + do * 128
                    ins = e.copy(
                        out=av[:, grp * 4:(grp + 1) * 4, fcol:fcol + 128],
                        in_=c.bank(2 + idx % 2).rearrange("p (q d) -> p q d", q=4))
                    v_apt[idx] = c.ps.inc(ins)

    s_r2 = {}
    v_x2 = {}
    s_r3 = {}
    v_l3 = {}
    t_l3t = {}
    v_l3fm = {}
    dx_xo = {}
    dx_sp = {}

    for p in range(8):
        @block.sync
        def _(e, p=p):
            if p == 0:
                e.wait_ge(c.pv.sem, c.v_att_all)
                e.wait_ge(c.pt.sem, bar_pt)
                e.wait_ge(c.ps.sem, bar_ps)
            else:
                e.wait_ge(c.pv.sem, v_x2[p - 1])
            e.wait_ge(c.dx.sem, c.dx.n)
            c.dx.inc(e.dma_start(out=xo[p % 2][:], in_=c.xown[p * 128:(p + 1) * 128, :]), 16)
            dx_xo[p] = c.dx.n

        @block.vector
        def _(e, p=p):
            need = max(v_apt[(h2 * 2 + p // 4) * 8 + do2]
                       for h2 in range(2) for do2 in range(8))
            e.wait_ge(c.ps.sem, need)
            av = aptok[:].rearrange("p (t d) -> p t d", t=8)
            c.pv.inc(_ln_stats_edge(c, e, av[:, p, :], 2048))

        @block.scalar
        def _(e, p=p):
            e.wait_ge(c.pv.sem, c.pv.n)
            s_r2[p] = c.ps.inc(_rstd_emit(c, e))

        @block.vector
        def _(e, p=p):
            e.wait_ge(c.ps.sem, s_r2[p])
            e.wait_ge(c.dx.sem, dx_xo[p])
            if p >= 1:
                e.wait_ge(c.pv.sem, v_l3[p - 1])
            av = aptok[:].rearrange("p (t d) -> p t d", t=8)
            c.pv.inc(e.tensor_scalar(out=x2t[p % 2][:], in0=av[:, p, :],
                                     scalar1=c.mv[:, 0:1], scalar2=c.rstd[:, 0:1],
                                     op0=OP.subtract, op1=OP.mult))
            e.wait_ge(c.pv.sem, c.pv.n)
            ins = e.tensor_add(x2t[p % 2][:], x2t[p % 2][:], xo[p % 2][:])
            v_x2[p] = c.pv.inc(ins)
            e.wait_ge(c.pv.sem, v_x2[p])
            c.pv.inc(_ln_stats_edge(c, e, x2t[p % 2][:], 2048))

        @block.sync
        def _(e, p=p):
            e.wait_ge(c.pv.sem, v_x2[p])
            e.wait_ge(c.dx.sem, c.dx.n)
            c.dx.inc(e.dma_start(out=c.x2scr[p * 128:(p + 1) * 128, :], in_=x2t[p % 2][:]), 16)
            dx_sp[p] = c.dx.n

        @block.scalar
        def _(e, p=p):
            e.wait_ge(c.pv.sem, c.pv.n)
            s_r3[p] = c.ps.inc(_rstd_emit(c, e))

        @block.vector
        def _(e, p=p):
            e.wait_ge(c.ps.sem, s_r3[p])
            ins = e.tensor_scalar(out=l3[p % 2][:], in0=x2t[p % 2][:],
                                  scalar1=c.mv[:, 0:1], scalar2=c.rstd[:, 0:1],
                                  op0=OP.subtract, op1=OP.mult)
            v_l3[p] = c.pv.inc(ins)

        @block.tensor
        def _(e, p=p):
            e.wait_ge(c.pv.sem, v_l3[p])
            if p >= 2:
                e.wait_ge(c.ps.sem, v_l3fm[p - 2])
            base = (p % 2) * 4
            for dm in range(16):
                ins = e.matmul(c.bank(base + dm // 4)[:, (dm % 4) * 128:(dm % 4 + 1) * 128],
                               l3[p % 2][:, dm * 128:(dm + 1) * 128], c.eye_b[:, :],
                               start=True, stop=True)
            t_l3t[p] = c.pt.inc(ins)

        @block.scalar
        def _(e, p=p):
            e.wait_ge(c.pt.sem, t_l3t[p])
            base = (p % 2) * 4
            lv = c.ln3_fm[:].rearrange("p (d o) -> p d o", d=16)
            for bg in range(4):
                ins = e.copy(
                    out=lv[:, bg * 4:(bg + 1) * 4, p * 128:(p + 1) * 128],
                    in_=c.bank(base + bg).rearrange("p (q d) -> p q d", q=4))
            v_l3fm[p] = c.ps.inc(ins)

    c.v_l3fm_all = v_l3fm[7]
    c.dx_sp = dx_sp

    @block.gpsimd
    def _(e):
        e.wait_ge(c.pt.sem, t_wo[31])
        e.wait_ge(c.dw.sem, c.dw.n)
        c.dw.inc(e.dma_start(out=c.wgt[0][:], in_=c.wg_p[0]), 16)
        e.wait_ge(c.dw.sem, c.dw.n)
        c.dw.inc(e.dma_start(out=c.wut[0][:], in_=c.wu_p[0]), 16)
        c.dw_pref = c.dw.n


def emit_dump_c(c):
    nc, block = c.nc, c.block

    @block.sync
    def _(e):
        e.wait_ge(c.ps.sem, c.v_l3fm_all)
        c.dx.inc(e.dma_start(out=c.d_ln3, in_=c.ln3_fm[:, :]), 16)
        for p in range(8):
            e.wait_ge(c.dx.sem, c.dx_sp[p])
        for p in range(8):
            c.dx.inc(e.dma_start(out=c.d_x2[p * 128:(p + 1) * 128, :],
                                 in_=c.x2scr[p * 128:(p + 1) * 128, :]), 16)
        e.wait_ge(c.dx.sem, c.dx.n)


def emit_phase_d(c):
    nc, block = c.nc, c.block
    wgt, wut, gtmp, gs, us, s3, sdn = c.wgt, c.wut, c.gtmp, c.gs, c.us, c.s3, c.sdn

    bar_pt = c.pt.n
    bar_ps = c.ps.n

    @block.scalar
    def _(e):
        _preload_set(c, e, [ACT.Silu])

    @block.tensor
    def _(e):
        e.wait_ge(c.ps.sem, c.v_l3fm_all)
        for hc in range(16):
            for half in range(2):
                ins = e.matmul(c.psA[0:1, half * 512:(half + 1) * 512],
                               c.onescb[:, 0:1],
                               c.ln3_fm[:, hc * 1024 + half * 512:hc * 1024 + (half + 1) * 512],
                               start=(hc == 0), stop=(hc == 15))
        c.pt.inc(ins)

    @block.vector
    def _(e):
        e.wait_ge(c.pt.sem, c.pt.n)
        e.wait_ge(c.dx.sem, c.dx.n)
        c.pv.inc(e.tensor_copy(out=s3[:], in_=c.psA[0:1, 0:1024]))
    v_s3 = c.pv.n

    @block.tensor
    def _(e):
        e.wait_ge(c.pv.sem, v_s3)
        for half in range(2):
            ins = e.matmul(c.psB[:, half * 512:(half + 1) * 512], c.ones1f[0:1, :],
                           s3[0:1, half * 512:(half + 1) * 512], start=True, stop=True)
        c.pt.inc(ins)

    @block.vector
    def _(e):
        e.wait_ge(c.pt.sem, c.pt.n)
        ins = e.tensor_scalar(out=c.cs3[:, :], in0=c.psB[:, 0:1024], scalar1=128.0,
                              scalar2=None, op0=OP.mult)
        c.pv.inc(ins)
    v_cs3 = c.pv.n

    dw_g = {}
    t_gu = {}
    s_ge = {}
    v_ue = {}
    v_h = {}

    dw_g[0] = c.dw_pref
    for ib in range(64):
        if ib >= 1:
            @block.gpsimd
            def _(e, ib=ib):
                if ib >= 2:
                    e.wait_ge(c.pt.sem, t_gu[ib - 2])
                e.wait_ge(c.dw.sem, c.dw.n)
                c.dw.inc(e.dma_start(out=wgt[ib % 2][:], in_=c.wg_p[ib]), 16)
                e.wait_ge(c.dw.sem, c.dw.n)
                ins = e.dma_start(out=wut[ib % 2][:], in_=c.wu_p[ib])
                dw_g[ib] = c.dw.inc(ins, 16)

        @block.tensor
        def _(e, ib=ib):
            e.wait_ge(c.dw.sem, dw_g[ib])
            if ib == 0:
                e.wait_ge(c.pv.sem, v_cs3)
            if ib >= 2:
                e.wait_ge(c.ps.sem, s_ge[ib - 2])
                e.wait_ge(c.pv.sem, v_ue[ib - 2])
            gb = ib % 2
            gbank = c.psA[:, gb * 1024:(gb + 1) * 1024]
            ubank = c.psB[:, gb * 1024:(gb + 1) * 1024]
            for half in range(2):
                for hc in range(16):
                    e.matmul(gbank[:, half * 512:(half + 1) * 512],
                             wgt[ib % 2][:, hc * 128:(hc + 1) * 128],
                             c.ln3_fm[:, hc * 1024 + half * 512:hc * 1024 + (half + 1) * 512],
                             start=(hc == 0), stop=(hc == 15))
                for hc in range(16):
                    ins = e.matmul(ubank[:, half * 512:(half + 1) * 512],
                                   wut[ib % 2][:, hc * 128:(hc + 1) * 128],
                                   c.ln3_fm[:, hc * 1024 + half * 512:hc * 1024 + (half + 1) * 512],
                                   start=(hc == 0), stop=(hc == 15))
            t_gu[ib] = c.pt.inc(ins)

        @block.vector
        def _(e, ib=ib):
            e.wait_ge(c.pt.sem, t_gu[ib])
            if ib >= 1:
                e.wait_ge(c.ps.sem, s_ge[ib - 1])
                e.wait_ge(c.pv.sem, v_h[ib - 1])
            gbank = c.psA[:, (ib % 2) * 1024:(ib % 2 + 1) * 1024]
            ubank = c.psB[:, (ib % 2) * 1024:(ib % 2 + 1) * 1024]
            ins = e.tensor_tensor(out=gtmp[:], in0=gbank, in1=c.cs3[:, :], op=OP.subtract)
            v_gsub = c.pv.inc(ins)
            ins = e.tensor_tensor(out=c.ubf[:], in0=ubank, in1=c.cs3[:, :], op=OP.subtract)
            v_ue[ib] = c.pv.inc(ins)

        @block.scalar
        def _(e, ib=ib):
            e.wait_ge(c.pv.sem, v_ue[ib])
            ins = e.activation(out=gs[:], in_=gtmp[:], func=ACT.Silu,
                               bias=0.0, scale=c.gsc_s[:, ib:ib + 1])
            s_ge[ib] = c.ps.inc(ins)

        @block.vector
        def _(e, ib=ib):
            e.wait_ge(c.ps.sem, s_ge[ib])
            ins = e.scalar_tensor_tensor(out=c.hdn[:, ib * 1024:(ib + 1) * 1024],
                                         in0=gs[:], scalar=c.usc_s[:, ib:ib + 1],
                                         in1=c.ubf[:], op0=OP.mult, op1=OP.mult)
            v_h[ib] = c.pv.inc(ins)

    @block.tensor
    def _(e):
        e.wait_ge(c.pv.sem, v_h[63])
        for ic in range(64):
            for half in range(2):
                ins = e.matmul(c.psA[0:1, half * 512:(half + 1) * 512],
                               c.onescb[:, 0:1],
                               c.hdn[:, ic * 1024 + half * 512:ic * 1024 + (half + 1) * 512],
                               start=(ic == 0), stop=(ic == 63))
        c.pt.inc(ins)

    @block.vector
    def _(e):
        e.wait_ge(c.pt.sem, c.pt.n)
        c.pv.inc(e.tensor_copy(out=sdn[:], in_=c.psA[0:1, 0:1024]))
    v_sdn = c.pv.n

    bar_d2_pt = c.pt.n
    wdt = c.wdt
    dw_d = {}
    t_dn = {}
    v_y = {}

    @block.sync
    def _(e):
        e.wait_ge(c.pt.sem, bar_d2_pt)
        e.wait_ge(c.ps.sem, c.ps.n)
        e.wait_ge(c.dx.sem, c.dx.n)
        c.dx.inc(e.dma_start(out=c.dsc_s[:], in_=c.dsc), 16)
    dx_dsc = c.dx.n

    @block.tensor
    def _(e):
        e.wait_ge(c.dx.sem, dx_dsc)
        e.wait_ge(c.pv.sem, v_sdn)
        for j in range(4):
            ins = e.matmul(c.bank(j), c.ones1f[0:1, :],
                           c.dsc_s[0:1, j * 512:(j + 1) * 512], start=True, stop=True)
        c.pt.inc(ins)

    @block.vector
    def _(e):
        e.wait_ge(c.pt.sem, c.pt.n)
        e.wait_ge(c.pv.sem, v_h[63])
        e.memset(c.m128f[:], -128.0)
        c.pv.inc(e.tensor_copy(out=c.dsc_bc[:], in_=c.psA[:, :]))
    v_dscbc = c.pv.n

    for hq in range(4):
        for icg in range(16):
            gi = hq * 16 + icg

            @block.gpsimd
            def _(e, hq=hq, icg=icg, gi=gi):
                if gi < 3:
                    e.wait_ge(c.pt.sem, bar_d2_pt)
                else:
                    e.wait_ge(c.pt.sem, t_dn[gi - 3])
                e.wait_ge(c.dw.sem, c.dw.n)
                ins = e.dma_start(out=wdt[gi % 3][:], in_=c.wd_p[hq, icg])
                dw_d[gi] = c.dw.inc(ins, 16)

            @block.tensor
            def _(e, hq=hq, icg=icg, gi=gi):
                e.wait_ge(c.dw.sem, dw_d[gi])
                if icg == 0:
                    if gi == 0:
                        e.wait_ge(c.pv.sem, v_dscbc)
                    else:
                        e.wait_ge(c.pv.sem, v_y[hq - 1])
                for ici in range(4):
                    ic = icg * 4 + ici
                    for tt in range(8):
                        ins = e.matmul(c.bank(tt),
                                       c.hdn[:, ic * 1024 + tt * 128:ic * 1024 + (tt + 1) * 128],
                                       wdt[gi % 3][:, ici * 512:(ici + 1) * 512],
                                       start=(ic == 0), stop=False, skip_group_check=True)
                if icg == 15:
                    for tt in range(8):
                        ins = e.matmul(c.bank(tt),
                                       sdn[0:1, tt * 128:(tt + 1) * 128],
                                       c.m128f[0:1, 0:512],
                                       start=False, stop=True, skip_group_check=True)
                t_dn[gi] = c.pt.inc(ins)

        @block.vector
        def _(e, hq=hq):
            e.wait_ge(c.pt.sem, t_dn[hq * 16 + 15])
            yv = c.y[:].rearrange("p (t d) -> p t d", t=8)
            for tt in range(8):
                ins = e.tensor_tensor(out=yv[:, tt, hq * 512:(hq + 1) * 512],
                                      in0=c.bank(tt),
                                      in1=c.dsc_bc[:, hq * 512:(hq + 1) * 512], op=OP.mult)
            v_y[hq] = c.pv.inc(ins)

    c.v_y_all = v_y[3]


def emit_phase_e(c):
    nc, block = c.nc, c.block
    x2r = c.x2r

    @block.scalar
    def _(e):
        _preload_set(c, e, [ACT.Exp, ACT.Ln])

    dx_r = {}
    s_r4 = {}
    v_o = {}

    for p in range(8):
        @block.sync
        def _(e, p=p):
            if p == 0:
                e.wait_ge(c.pv.sem, c.v_y_all)
                for pp in range(8):
                    e.wait_ge(c.dx.sem, c.dx_sp[pp])
            if p >= 2:
                e.wait_ge(c.pv.sem, v_o[p - 2])
            e.wait_ge(c.dx.sem, c.dx.n)
            c.dx.inc(e.dma_start(out=x2r[p % 2][:], in_=c.x2scr[p * 128:(p + 1) * 128, :]), 16)
            dx_r[p] = c.dx.n

        @block.vector
        def _(e, p=p):
            e.wait_ge(c.pv.sem, c.v_y_all)
            yv = c.y[:].rearrange("p (t d) -> p t d", t=8)
            c.pv.inc(_ln_stats_edge(c, e, yv[:, p, :], 2048))

        @block.scalar
        def _(e, p=p):
            e.wait_ge(c.pv.sem, c.pv.n)
            s_r4[p] = c.ps.inc(_rstd_emit(c, e))

        @block.vector
        def _(e, p=p):
            e.wait_ge(c.ps.sem, s_r4[p])
            e.wait_ge(c.dx.sem, dx_r[p])
            yv = c.y[:].rearrange("p (t d) -> p t d", t=8)
            for half in range(2):
                c.pv.inc(e.tensor_scalar(
                    out=c.ln4b[:], in0=yv[:, p, half * 1024:(half + 1) * 1024],
                    scalar1=c.mv[:, 0:1], scalar2=c.rstd[:, 0:1],
                    op0=OP.subtract, op1=OP.mult))
                e.wait_ge(c.pv.sem, c.pv.n)
                sl = x2r[p % 2][:, half * 1024:(half + 1) * 1024]
                v_o[p] = c.pv.inc(e.tensor_add(sl, sl, c.ln4b[:]))
                if half == 0:
                    e.wait_ge(c.pv.sem, v_o[p])

        @block.sync
        def _(e, p=p):
            e.wait_ge(c.pv.sem, v_o[p])
            e.wait_ge(c.dx.sem, c.dx.n)
            c.dx.inc(e.dma_start(out=c.out[p * 128:(p + 1) * 128, :], in_=x2r[p % 2][:]), 16)

    @block.sync
    def _(e):
        e.wait_ge(c.dx.sem, c.dx.n)


def make_in_maps(x, wq, wk, wv, wo, gw, gscale, uw, uscale, dw, dscale):
    common = {}
    common["wq_p"] = np.ascontiguousarray(wq.reshape(16, 128, 2048).transpose(1, 0, 2))
    common["wk_p"] = np.ascontiguousarray(wk.reshape(16, 128, 512).transpose(1, 0, 2))
    common["wv_p"] = np.ascontiguousarray(wv.reshape(16, 128, 512).transpose(1, 0, 2))
    common["wo_p"] = np.ascontiguousarray(wo.reshape(16, 128, 2048).transpose(1, 0, 2))
    wgT = np.ascontiguousarray(gw.T)
    wuT = np.ascontiguousarray(uw.T)
    common["wg_p"] = np.ascontiguousarray(
        wgT.reshape(16, 128, 64, 128).transpose(2, 1, 0, 3).reshape(64, 128, 2048))
    common["wu_p"] = np.ascontiguousarray(
        wuT.reshape(16, 128, 64, 128).transpose(2, 1, 0, 3).reshape(64, 128, 2048))
    wdT = np.ascontiguousarray(dw.T)
    common["wd_p"] = np.ascontiguousarray(
        wdT.reshape(16, 4, 128, 4, 512).transpose(3, 0, 2, 1, 4).reshape(4, 16, 128, 2048))
    common["gsc"] = np.ascontiguousarray(gscale.reshape(64, 128).T)
    common["usc"] = np.ascontiguousarray(uscale.reshape(64, 128).T)
    common["dsc"] = np.ascontiguousarray(dscale.reshape(1, 2048))
    common["eye"] = np.eye(128, dtype=np.float32)
    colh = np.arange(256) // 8
    sel = (np.arange(8)[:, None] == (colh[None, :] // 4)).astype(np.float32)
    common["selmap"] = sel

    import jax
    import jax.numpy as jnp
    with jax.default_device(jax.devices("cpu")[0]):
        inv_freq = 1.0 / (10000.0 ** (jnp.arange(0, HD, 2, dtype=jnp.float32) / HD))
        pos_j = jnp.arange(SEQ, dtype=jnp.float32)
        freqs = pos_j[:, None] * inv_freq[None, :]
        cosf = np.asarray(jnp.cos(freqs)).astype(np.float32)
        sinf = np.asarray(jnp.sin(freqs)).astype(np.float32)

    rr = np.arange(128)
    in_maps = []
    for core in range(8):
        b, delta = core // 2, core % 2
        m = dict(common)
        perm = np.where(rr < 64, rr + 64, rr - 64) if delta else rr.copy()
        gperm = (np.arange(16)[:, None] * 128 + perm[None, :]).ravel()
        m["xb"] = np.ascontiguousarray(x[b][gperm])
        ownrows = (np.arange(16)[:, None] * 128 + delta * 64 + np.arange(64)[None, :]).ravel()
        m["xown"] = np.ascontiguousarray(x[b][ownrows])
        pp = np.arange(8)[None, :]
        posq = (2 * pp + (rr[:, None] >= 64)) * 128 + delta * 64 + (rr[:, None] % 64)
        m["qcos"] = np.ascontiguousarray((cosf[posq] / 8.0).astype(np.float32).reshape(128, 256))
        m["qsin"] = np.ascontiguousarray((sinf[posq] / 8.0).astype(np.float32).reshape(128, 256))
        post = np.arange(16)[None, :] * 128 + perm[:, None]
        m["kcos"] = np.ascontiguousarray(cosf[post].astype(np.float32).reshape(128, 512))
        m["ksin"] = np.ascontiguousarray(sinf[post].astype(np.float32).reshape(128, 512))
        kk = perm[:, None].astype(np.int64)
        col = np.arange(128)[None, :]
        qrel = delta * 64 + (col % 64)
        mA = np.ones((128, 128), np.float32)
        mA[:, 0:64] = np.where(kk > qrel[:, 0:64], np.float32(0.0), np.float32(1.0))
        mB = np.zeros((128, 128), np.float32)
        mB[:, 64:128] = np.where(kk > qrel[:, 64:128], np.float32(0.0), np.float32(1.0))
        m["maskA"] = mA.astype(ml_dtypes.bfloat16)
        m["maskB"] = mB.astype(ml_dtypes.bfloat16)
        in_maps.append(m)
    return in_maps


def kernel(L0_attn_x, L0_attn_wq, L0_attn_wk, L0_attn_wv, L0_attn_wo,
           L0_ffn_gate_w, L0_ffn_gate_scale, L0_ffn_up_w, L0_ffn_up_scale,
           L0_ffn_down_w, L0_ffn_down_scale):
    x = np.asarray(L0_attn_x, dtype=np.float32)
    in_maps = make_in_maps(x,
                           np.asarray(L0_attn_wq, np.float32),
                           np.asarray(L0_attn_wk, np.float32),
                           np.asarray(L0_attn_wv, np.float32),
                           np.asarray(L0_attn_wo, np.float32),
                           np.asarray(L0_ffn_gate_w, np.int32),
                           np.asarray(L0_ffn_gate_scale, np.float32),
                           np.asarray(L0_ffn_up_w, np.int32),
                           np.asarray(L0_ffn_up_scale, np.float32),
                           np.asarray(L0_ffn_down_w, np.int32),
                           np.asarray(L0_ffn_down_scale, np.float32))
    nc = build_nc("full")
    res = run_bass_kernel_spmd(nc, in_maps, core_ids=list(range(8)))
    out = np.empty((4, SEQ, DIM), dtype=np.float32)
    for core in range(8):
        b, delta = core // 2, core % 2
        ownrows = (np.arange(16)[:, None] * 128 + delta * 64 + np.arange(64)[None, :]).ravel()
        out[b][ownrows] = res.results[core]["out"]
    return out


# version 17
# speedup vs baseline: 1.0060x; 1.0060x over previous
import os
import ml_dtypes
import numpy as np
from contextlib import ExitStack

import concourse.bass as bass
import concourse.mybir as mybir
from concourse.bass_utils import run_bass_kernel_spmd

F32 = mybir.dt.float32
BF16 = mybir.dt.bfloat16
I32 = mybir.dt.int32
AX = mybir.AxisListType.X
OP = mybir.AluOpType
ACT = mybir.ActivationFunctionType

DIM = 2048
SEQ = 2048
HD = 64
EPS = 1e-5

NQ_OF_C = [(8 - cc // 2) * 128 for cc in range(16)]
OFFC = [0]
for _n in NQ_OF_C:
    OFFC.append(OFFC[-1] + _n)


class Prog:
    def __init__(self, sem):
        self.sem = sem
        self.n = 0

    def inc(self, ins, k=1):
        ins.then_inc(self.sem, k)
        self.n += k
        return self.n


class Ctx:
    pass


def build_nc(stage="full"):
    nc = bass.Bass("TRN2", target_bir_lowering=False)
    c = Ctx()
    c.nc = nc
    c.stage = stage
    din = lambda name, shape, dt: nc.dram_tensor(name, list(shape), dt, kind="ExternalInput").ap()

    c.xb = din("xb", [SEQ, DIM], F32)
    c.xown = din("xown", [1024, DIM], F32)
    c.wq_p = din("wq_p", [128, 16, 2048], F32)
    c.wk_p = din("wk_p", [128, 16, 512], F32)
    c.wv_p = din("wv_p", [128, 16, 512], F32)
    c.wo_p = din("wo_p", [128, 16, 2048], F32)
    c.wg_p = din("wg_p", [64, 128, 2048], I32)
    c.wu_p = din("wu_p", [64, 128, 2048], I32)
    c.wd_p = din("wd_p", [4, 16, 128, 2048], I32)
    c.gsc = din("gsc", [128, 64], F32)
    c.usc = din("usc", [128, 64], F32)
    c.dsc = din("dsc", [1, 2048], F32)
    c.qcos = din("qcos", [128, 256], F32)
    c.qsin = din("qsin", [128, 256], F32)
    c.kcos = din("kcos", [128, 512], F32)
    c.ksin = din("ksin", [128, 512], F32)
    c.maskA = din("maskA", [128, 128], BF16)
    c.maskB = din("maskB", [128, 128], BF16)
    c.selmap = din("selmap", [8, 256], F32)
    c.eye = din("eye", [128, 128], F32)

    c.out = nc.dram_tensor("out", [1024, DIM], F32, kind="ExternalOutput").ap()
    c.x2scr = nc.dram_tensor("x2scr", [1024, DIM], F32).ap()

    if stage == "A":
        c.d_k65 = nc.dram_tensor("d_k65", [65, 16384], BF16, kind="ExternalOutput").ap()
        c.d_v65 = nc.dram_tensor("d_v65", [128, 8320], BF16, kind="ExternalOutput").ap()
        c.d_q65 = nc.dram_tensor("d_q65", [65, 32768], BF16, kind="ExternalOutput").ap()
    if stage == "B":
        c.d_att = nc.dram_tensor("d_att", [128, 16384], BF16, kind="ExternalOutput").ap()
    if stage == "C":
        c.d_x2 = nc.dram_tensor("d_x2", [1024, DIM], F32, kind="ExternalOutput").ap()
        c.d_ln3 = nc.dram_tensor("d_ln3", [128, 16384], BF16, kind="ExternalOutput").ap()

    with ExitStack() as top:
        c.top = top
        sem = lambda name: top.enter_context(nc.semaphore(name))
        c.pt = Prog(sem("pt"))
        c.pv = Prog(sem("pv"))
        c.ps = Prog(sem("ps"))
        c.dx = Prog(sem("dx"))
        c.dw = Prog(sem("dw"))
        c.block = top.enter_context(nc.Block())

        sb = lambda name, shape, dt: top.enter_context(nc.sbuf_tensor(name, shape, dt))
        c.eye_b = sb("eye_b", [128, 128], BF16)
        c.eye_f = sb("eye_f", [128, 128], F32)
        c.mA = sb("mA", [128, 128], BF16)
        c.mB = sb("mB", [128, 128], BF16)
        c.ones1f = sb("ones1f", [1, 128], F32)
        c.ones1b = sb("ones1b", [1, 128], BF16)
        c.onescb = sb("onescb", [128, 1], BF16)
        c.stat6 = sb("stat6", [128, 24], F32)
        c.mv = sb("mv", [128, 2], F32)
        c.rstd = sb("rstd", [128, 1], F32)
        c.lntmp = sb("lntmp", [128, 1], F32)
        c.epsv = sb("epsv", [128, 1], F32)
        c.knm = sb("knm", [128, 8], F32)
        c.kn8 = sb("kn8", [128, 8], F32)
        c.kc = sb("kc", [8, 1], F32)
        c.cper = sb("cper", [128, 1], F32)
        c.rowv = sb("rowv", [128, 128], BF16)
        c.rden = sb("rden", [1, 1024], BF16)

        NBELS = 92288
        NFELS = 5632
        c.slb = top.enter_context(nc.sbuf_tensor("slb", [128, NBELS], BF16))
        c.slf = top.enter_context(nc.sbuf_tensor("slf", [128, NFELS], F32))

        def viewb(off, cols, parts=128):
            return c.slb[0:parts, off:off + cols]

        def viewf(off, cols, parts=128):
            return c.slf[0:parts, off:off + cols]

        c.k65 = viewb(0, 16384, parts=65)
        c.v65 = viewb(16384, 8320)
        c.l1 = [viewb(24704 + 2048 * t, 2048) for t in range(12)]
        c.fm = [viewb(49280, 2048), viewb(51328, 2048)]
        c.krot = viewb(53376, 512)
        c.q65 = viewb(24704, 32768, parts=65)
        c.wk_b = viewb(57472, 8192)
        c.wv_b = viewb(65664, 8192)
        c.wqh = viewb(57472, 16384)
        c.att_fm = viewb(57472, 16384)
        c.pairfm = viewb(73856, 16384)
        c.qrot = [viewb(90240, 512), viewb(90752, 512)]
        c.pbuf = [viewb(73856, 9216), viewb(83072, 9216)]
        c.xt = [viewf(0, 2048), viewf(2048, 2048)]
        c.rt1 = viewf(4096, 512)
        c.rt2 = viewf(4608, 512)
        c.kcos_s = viewf(5120, 256)
        c.ksin_s = viewf(5376, 256)
        c.qcos_s = viewf(0, 256)
        c.qsin_s = viewf(256, 256)
        c.qt1 = viewf(4096, 512)
        c.qt2 = viewf(4608, 512)
        c.qn2 = viewf(2048, 256)
        c.selm = viewf(2304, 256, parts=8)
        c.sqt = viewf(4096, 128)
        c.sq2 = viewf(4224, 128)
        c.dens = viewf(0, 1024, parts=1)
        c.lntmp2 = viewf(1024, 1024, parts=1)
        c.rbs2 = [c.slf[0:64, 2560:3584], c.slf[64:128, 2560:3584]]
        c.ln3_fm = viewb(0, 16384)
        c.woh = viewb(16384, 16384)
        c.woh2 = viewb(73856, 16384)
        c.aptok = viewb(32768, 16384)
        c.l3 = [viewb(49152, 2048), viewb(51200, 2048)]
        c.apf = [viewf(4096, 512), viewf(4608, 512)]
        c.xo = [viewf(0, 2048)] * 2
        c.x2t = [viewf(2048, 2048)] * 2
        c.gsc_s = viewf(5120, 64)
        c.usc_s = viewf(5184, 64)
        c.hdn = viewb(16384, 65536)
        c.wgt = [viewb(81920, 2048), viewb(83968, 2048)]
        c.wut = [viewb(86016, 2048), viewb(88064, 2048)]
        c.wdt = [viewb(81920, 2048), viewb(83968, 2048), viewb(86016, 2048)]
        c.gs = viewb(90112, 1024)
        c.ubf = viewb(91136, 1024)
        c.us = c.ubf
        c.y = viewb(0, 16384)
        c.gtmp = viewf(0, 1024)
        c.cs3 = viewf(2048, 1024)
        c.s3 = viewf(3072, 1024, parts=1)
        c.sdn = viewf(3072, 1024, parts=1)
        c.m128f = viewf(2048, 512, parts=1)
        c.dsc_s = viewf(0, 2048, parts=1)
        c.dsc_bc = viewf(0, 2048)
        c.x2r = [viewf(0, 2048), viewf(2048, 2048)]
        c.ln4b = viewf(4096, 1024)

        c.psA = top.enter_context(nc.psum_tensor("psA", [128, 2048], F32))
        c.psB = top.enter_context(nc.psum_tensor("psB", [128, 2048], F32))

        def bank(i, parts=128):
            t = c.psA if i < 4 else c.psB
            return t[0:parts, (i % 4) * 512:(i % 4) * 512 + 512]

        c.bank = bank

        emit_prelude(c)
        emit_phase_a(c)
        emit_phase_a2(c)
        if stage == "A":
            emit_dump_a(c)
        else:
            emit_phase_b(c)
            if stage == "B":
                emit_dump_b(c)
            else:
                emit_phase_c(c)
                if stage == "C":
                    emit_dump_c(c)
                else:
                    emit_phase_d(c)
                    emit_phase_e(c)
    return nc



def _preload_set(c, e, funcs):
    from concourse.hw_specs import get_activation_tables
    for idx, fns in enumerate(get_activation_tables(c.nc.m.arch).values()):
        if all(f in fns for f in funcs):
            return e.add_instruction(mybir.InstLoadActFuncSet(
                name=c.nc.get_next_instruction_name(),
                act_func_set_id=idx, ins=[], outs=[]))
    raise ValueError(f"no activation set contains {funcs}")


def _ln_stats_edge(c, e, src_ap, n_free):
    ngrp = (n_free + 511) // 512
    for g in range(ngrp):
        ins = e.bn_stats(out=c.stat6[:, g * 6:(g + 1) * 6],
                         in_=src_ap[:, g * 512:min((g + 1) * 512, n_free)])
    c.pv.inc(ins)
    e.wait_ge(c.pv.sem, c.pv.n)
    return e.bn_aggr(out=c.mv[:], in_=c.stat6[:, 0:6 * ngrp])


def _rstd_emit(c, e):
    c.ps.inc(e.activation(out=c.lntmp[:], in_=c.mv[:, 1:2], func=ACT.Ln,
                          bias=c.epsv[:, 0:1], scale=1.0))
    e.wait_ge(c.ps.sem, c.ps.n)
    return e.activation(out=c.rstd[:], in_=c.lntmp[:], func=ACT.Exp,
                        bias=0.0, scale=-0.5)


def _rope(c, e, psum_src, nheads, cos_ap, sin_ap, dest_ap, t1, t2):
    src = psum_src.rearrange("p (h j two) -> p h j two", h=nheads, j=32, two=2)
    ev = src[:, :, :, 0]
    ov = src[:, :, :, 1]
    cb = cos_ap[:, None, :].broadcast_to((128, nheads, 32))
    sbb = sin_ap[:, None, :].broadcast_to((128, nheads, 32))
    t1v = t1[:, 0:nheads * 32].rearrange("p (h j) -> p h j", h=nheads)
    t2v = t2[:, 0:nheads * 32].rearrange("p (h j) -> p h j", h=nheads)
    dv = dest_ap.rearrange("p (h half j) -> p h half j", h=nheads, half=2, j=32)
    e.tensor_tensor(out=t1v, in0=ev, in1=cb, op=OP.mult)
    c.pv.inc(e.tensor_tensor(out=t2v, in0=ov, in1=sbb, op=OP.mult))
    e.wait_ge(c.pv.sem, c.pv.n)
    c.pv.inc(e.tensor_sub(dv[:, :, 0, :], t1v, t2v))
    e.wait_ge(c.pv.sem, c.pv.n)
    e.tensor_tensor(out=t1v, in0=ev, in1=sbb, op=OP.mult)
    c.pv.inc(e.tensor_tensor(out=t2v, in0=ov, in1=cb, op=OP.mult))
    e.wait_ge(c.pv.sem, c.pv.n)
    return e.tensor_add(dv[:, :, 1, :], t1v, t2v)


def emit_prelude(c):
    block = c.block

    @block.sync
    def _(e):
        for dst, srcap in [(c.eye_f, c.eye), (c.mA, c.maskA), (c.mB, c.maskB),
                           (c.kcos_s, c.kcos[:, 0:256]),
                           (c.ksin_s, c.ksin[:, 0:256])]:
            c.dx.inc(e.dma_start(out=dst[:], in_=srcap), 16)
        e.wait_ge(c.dx.sem, c.dx.n)

    @block.gpsimd
    def _(e):
        c.dw.inc(e.dma_start(out=c.eye_b[:], in_=c.eye), 16)

    @block.vector
    def _(e):
        e.memset(c.ones1f[:], 1.0)
        e.memset(c.ones1b[:], 1.0)
        e.memset(c.epsv[:], EPS)
        c.pv.inc(e.memset(c.onescb[:], 1.0))
    c.v_pre = c.pv.n

    @block.scalar
    def _(e):
        _preload_set(c, e, [ACT.Exp, ACT.Ln])


def emit_phase_a(c):
    nc, block = c.nc, c.block
    xt, l1, fm, krot, rt1, rt2 = c.xt, c.l1, c.fm, c.krot, c.rt1, c.rt2

    @block.vector
    def _(e):
        e.memset(c.k65[64:65, :], 1.0)
        v = c.v65[:].rearrange("p (t h s) -> p t h s", t=16, h=8, s=65)
        c.pv.inc(e.memset(v[:, :, :, 64:65], 1.0))

    @block.gpsimd
    def _(e):
        e.wait_ge(c.dw.sem, 16)
        c.dw.inc(e.dma_start(out=c.wk_b[:].rearrange("p (a b) -> p a b", a=16),
                             in_=c.wk_p), 16)
        e.wait_ge(c.dw.sem, 32)
        c.dw.inc(e.dma_start(out=c.wv_b[:].rearrange("p (a b) -> p a b", a=16),
                             in_=c.wv_p), 16)
    c.dw_kv = c.dw.n

    v_ln1 = {}
    v_fmev = {}
    v_krope = {}
    v_vev = {}
    v_ktev = {}
    t_tr = {}
    t_kv = {}
    t_kt = {}
    s_rstd = {}
    dx_x = {}

    def emit_a0(t):
        @block.sync
        def _(e, t=t):
            if t >= 2:
                e.wait_ge(c.pv.sem, v_ln1[t - 2])
            if t >= 1:
                e.wait_ge(c.dx.sem, c.dx.n)
            c.dx.inc(e.dma_start(out=xt[t % 2][:], in_=c.xb[t * 128:(t + 1) * 128, :]), 16)
            dx_x[t] = c.dx.n

        @block.vector
        def _(e, t=t):
            if t == 0:
                e.wait_ge(c.pv.sem, c.v_pre)
            e.wait_ge(c.dx.sem, dx_x[t])
            c.pv.inc(_ln_stats_edge(c, e, xt[t % 2][:], 2048))

        @block.scalar
        def _(e, t=t):
            e.wait_ge(c.pv.sem, c.pv.n)
            s_rstd[t] = c.ps.inc(_rstd_emit(c, e))

        @block.vector
        def _(e, t=t):
            e.wait_ge(c.ps.sem, s_rstd[t])
            if t >= 12:
                e.wait_ge(c.pt.sem, t_tr[t - 12])
            ins = e.tensor_scalar(out=l1[t % 12][:], in0=xt[t % 2][:],
                                  scalar1=c.mv[:, 0:1], scalar2=c.rstd[:, 0:1],
                                  op0=OP.subtract, op1=OP.mult)
            v_ln1[t] = c.pv.inc(ins)

    def emit_a1(t):
        @block.tensor
        def _(e, t=t):
            e.wait_ge(c.pv.sem, v_ln1[t])
            if t >= 1:
                e.wait_ge(c.ps.sem, v_fmev[t - 1])
            e.wait_ge(c.dw.sem, c.dw_kv)
            for dm in range(16):
                ins = e.matmul(c.bank(dm // 4)[:, (dm % 4) * 128:(dm % 4 + 1) * 128],
                               l1[t % 12][:, dm * 128:(dm + 1) * 128],
                               c.eye_b[:, :], start=True, stop=True)
            t_tr[t] = c.pt.inc(ins)

        @block.scalar
        def _(e, t=t):
            e.wait_ge(c.pt.sem, t_tr[t])
            for q in range(4):
                e.copy(out=fm[t % 2][:, q * 512:(q + 1) * 512], in_=c.bank(q))
            pfv = c.pairfm[:].rearrange("p (d o) -> p d o", d=16)
            for bg in range(4):
                sl = c.bank(bg).rearrange("p (q cc) -> p q cc", q=4)
                ins = e.copy(out=pfv[:, bg * 4:(bg + 1) * 4, t * 64:t * 64 + 64],
                             in_=sl[:, :, 0:64])
            v_fmev[t] = c.ps.inc(ins)

        @block.tensor
        def _(e, t=t):
            e.wait_ge(c.ps.sem, v_fmev[t])
            if t >= 1:
                e.wait_ge(c.pv.sem, v_krope[t - 1])
                e.wait_ge(c.pv.sem, v_vev[t - 1])
            for dm in range(16):
                e.matmul(c.bank(4), fm[t % 2][:, dm * 128:(dm + 1) * 128],
                         c.wk_b[:, dm * 512:(dm + 1) * 512],
                         start=(dm == 0), stop=(dm == 15))
                ins = e.matmul(c.bank(5), fm[t % 2][:, dm * 128:(dm + 1) * 128],
                               c.wv_b[:, dm * 512:(dm + 1) * 512],
                               start=(dm == 0), stop=(dm == 15))
            t_kv[t] = c.pt.inc(ins)

        @block.vector
        def _(e, t=t):
            e.wait_ge(c.pt.sem, t_kv[t])
            if t == 8:
                e.wait_ge(c.dx.sem, c.dx_kh)
            tm = t % 8
            c.pv.inc(_rope(c, e, c.bank(4), 8, c.kcos_s[:, tm * 32:(tm + 1) * 32],
                           c.ksin_s[:, tm * 32:(tm + 1) * 32], krot[:], rt1, rt2))
            e.wait_ge(c.pv.sem, c.pv.n)
            c.pv.inc(e.tensor_mul(rt1[:], krot[:], krot[:]))
            e.wait_ge(c.pv.sem, c.pv.n)
            if t == 0:
                ins = e.reduce_sum(out=c.knm[:, :],
                                   in_=rt1[:].rearrange("p (h d) -> p h d", h=8), axis=AX)
            else:
                ins = e.reduce_sum(out=c.kn8[:, :],
                                   in_=rt1[:].rearrange("p (h d) -> p h d", h=8), axis=AX)
                c.pv.inc(ins)
                e.wait_ge(c.pv.sem, c.pv.n)
                ins = e.tensor_max(c.knm[:, :], c.knm[:, :], c.kn8[:, :])
            v_krope[t] = c.pv.inc(ins)
            vv = c.v65[:].rearrange("p (tt h s) -> p tt h s", tt=16, h=8, s=65)
            ins = e.tensor_copy(out=vv[:, t, :, 0:64],
                                in_=c.bank(5).rearrange("p (h d) -> p h d", h=8))
            v_vev[t] = c.pv.inc(ins)

        @block.tensor
        def _(e, t=t):
            e.wait_ge(c.pv.sem, v_krope[t])
            if t >= 1:
                e.wait_ge(c.ps.sem, v_ktev[t - 1])
            for hp in range(4):
                ins = e.matmul(c.bank(6 + hp // 2)[:, (hp % 2) * 128:(hp % 2 + 1) * 128],
                               krot[:, hp * 128:(hp + 1) * 128], c.eye_b[:, :],
                               start=True, stop=True)
            t_kt[t] = c.pt.inc(ins)

        @block.scalar
        def _(e, t=t):
            e.wait_ge(c.pt.sem, t_kt[t])
            for hp in range(4):
                bk = c.bank(6 + hp // 2)[:, (hp % 2) * 128:(hp % 2 + 1) * 128]
                e.copy(
                    out=c.k65[0:64, (2 * hp) * 2048 + t * 128:(2 * hp) * 2048 + (t + 1) * 128],
                    in_=bk[0:64, :])
                ins = e.copy(
                    out=c.k65[0:64, (2 * hp + 1) * 2048 + t * 128:(2 * hp + 1) * 2048 + (t + 1) * 128],
                    in_=bk[64:128, :])
            v_ktev[t] = c.ps.inc(ins)

        if t == 7:
            @block.sync
            def _(e):
                e.wait_ge(c.pv.sem, v_krope[7])
                e.wait_ge(c.dx.sem, c.dx.n)
                c.dx.inc(e.dma_start(out=c.kcos_s[:], in_=c.kcos[:, 256:512]), 16)
                c.dx.inc(e.dma_start(out=c.ksin_s[:], in_=c.ksin[:, 256:512]), 16)
                c.dx_kh = c.dx.n

    for t in range(18):
        if t < 16:
            emit_a0(t)
        if t >= 2:
            emit_a1(t - 2)

    c.v_fmev_last = v_fmev[15]
    c.s_a_last = c.ps.n


def emit_phase_a2(c):
    nc, block = c.nc, c.block
    bar_pt, bar_pv = c.pt.n, c.pv.n
    wqh, qrot, qt1, qt2 = c.wqh, c.qrot, c.qt1, c.qt2

    t_q = {}
    v_qr = {}
    t_qt = {}
    v_qte = {}
    dwq = {}

    for h in range(2):
        @block.gpsimd
        def _(e, h=h):
            if h == 0:
                e.wait_ge(c.pt.sem, bar_pt)
                e.wait_ge(c.pv.sem, bar_pv)
            if h == 1:
                e.wait_ge(c.pt.sem, t_q[(0, 7, 1)])
            e.wait_ge(c.dw.sem, c.dw.n)
            if h == 0:
                c.dw.inc(e.dma_start(out=c.qcos_s[:], in_=c.qcos), 16)
                c.dw.inc(e.dma_start(out=c.qsin_s[:], in_=c.qsin), 16)
                c.dw.inc(e.dma_start(out=c.selm[:], in_=c.selmap), 16)
                e.wait_ge(c.dw.sem, c.dw.n)
            ins = e.dma_start(out=wqh[:].rearrange("p (a b) -> p a b", a=16),
                              in_=c.wq_p[:, :, h * 1024:(h + 1) * 1024])
            dwq[h] = c.dw.inc(ins, 16)

        for p in range(8):
            for ob in range(2):
                key_prev = (h, p - 1, ob) if p >= 1 else ((0, 7, ob) if h == 1 else None)

                @block.tensor
                def _(e, h=h, p=p, ob=ob, key_prev=key_prev):
                    e.wait_ge(c.dw.sem, dwq[h])
                    if key_prev is None:
                        e.wait_ge(c.ps.sem, c.v_fmev_last)
                    else:
                        e.wait_ge(c.pv.sem, v_qr[key_prev])
                    for dm in range(16):
                        ins = e.matmul(c.bank(ob),
                                       c.pairfm[:, dm * 1024 + p * 128:dm * 1024 + (p + 1) * 128],
                                       wqh[:, dm * 1024 + ob * 512:dm * 1024 + (ob + 1) * 512],
                                       start=(dm == 0), stop=(dm == 15))
                    t_q[(h, p, ob)] = c.pt.inc(ins)

                @block.vector
                def _(e, h=h, p=p, ob=ob):
                    e.wait_ge(c.pt.sem, t_q[(h, p, ob)])
                    c.pv.inc(_rope(c, e, c.bank(ob), 8,
                                   c.qcos_s[:, p * 32:(p + 1) * 32],
                                   c.qsin_s[:, p * 32:(p + 1) * 32], qrot[ob][:], qt1, qt2))
                    e.wait_ge(c.pv.sem, c.pv.n)
                    c.pv.inc(e.tensor_mul(qt1[:], qrot[ob][:], qrot[ob][:]))
                    e.wait_ge(c.pv.sem, c.pv.n)
                    s0 = (h * 16 + ob * 8) * 8 + p
                    ins = e.reduce_sum(out=c.qn2[:, s0:s0 + 57:8],
                                       in_=qt1[:].rearrange("p (hh d) -> p hh d", hh=8),
                                       axis=AX)
                    v_qr[(h, p, ob)] = c.pv.inc(ins)

                @block.tensor
                def _(e, h=h, p=p, ob=ob, key_prev=key_prev):
                    e.wait_ge(c.pv.sem, v_qr[(h, p, ob)])
                    if key_prev is not None:
                        e.wait_ge(c.ps.sem, v_qte[key_prev])
                    else:
                        e.wait_ge(c.ps.sem, c.s_a_last)
                    for hp in range(4):
                        ins = e.matmul(
                            c.bank(4 + 2 * ob + hp // 2)[:, (hp % 2) * 128:(hp % 2 + 1) * 128],
                            qrot[ob][:, hp * 128:(hp + 1) * 128], c.eye_b[:, :],
                            start=True, stop=True)
                    t_qt[(h, p, ob)] = c.pt.inc(ins)

                @block.scalar
                def _(e, h=h, p=p, ob=ob):
                    e.wait_ge(c.pt.sem, t_qt[(h, p, ob)])
                    base_head = h * 16 + ob * 8
                    for hp in range(4):
                        bk = c.bank(4 + 2 * ob + hp // 2)[:, (hp % 2) * 128:(hp % 2 + 1) * 128]
                        q0 = base_head + 2 * hp
                        e.copy(
                            out=c.q65[0:64, q0 * 1024 + p * 128:q0 * 1024 + (p + 1) * 128],
                            in_=bk[0:64, :])
                        ins = e.copy(
                            out=c.q65[0:64, (q0 + 1) * 1024 + p * 128:(q0 + 1) * 1024 + (p + 1) * 128],
                            in_=bk[64:128, :])
                    v_qte[(h, p, ob)] = c.ps.inc(ins)

    c.v_a2_last = c.pv.n
    c.s_a2_last = c.ps.n
    c.t_a2_last = c.pt.n


def emit_dump_a(c):
    nc, block = c.nc, c.block

    @block.sync
    def _(e):
        e.wait_ge(c.pv.sem, c.v_a2_last)
        e.wait_ge(c.ps.sem, c.s_a2_last)
        c.dx.inc(e.dma_start(out=c.d_k65, in_=c.k65[:, :]), 16)
        c.dx.inc(e.dma_start(out=c.d_v65, in_=c.v65[:, :]), 16)
        c.dx.inc(e.dma_start(out=c.d_q65, in_=c.q65[:, :]), 16)
        e.wait_ge(c.dx.sem, c.dx.n)


def emit_phase_b(c):
    nc, block = c.nc, c.block
    pbuf = c.pbuf

    s2 = [c.psA[:, 0:1024], c.psA[:, 1024:2048], c.psB[:, 0:1024]]
    pvb1 = [c.bank(6, parts=65), c.bank(7, parts=65)]

    stab = {}
    stab_lvl = int(os.environ.get("ATH_STAB", "6"))
    if stab_lvl == 0:
        c.v_att_all = c.pv.n
        return

    @block.tensor
    def _(e):
        e.wait_ge(c.pv.sem, c.v_a2_last)
        e.wait_ge(c.ps.sem, c.s_a2_last)
        ins = e.matmul(c.psA[0:8, 0:128], c.knm[:, :], c.eye_f[:, :],
                       start=True, stop=True)
        stab["t_kt"] = c.pt.inc(ins)

    @block.vector
    def _(e):
        e.wait_ge(c.pt.sem, stab["t_kt"])
        stab["v_kc2"] = c.pv.inc(e.reduce_max(out=c.kc[:, 0:1], in_=c.psA[0:8, 0:128], axis=AX))

    @block.scalar
    def _(e):
        e.wait_ge(c.pv.sem, stab["v_kc2"])
        c.ps.inc(e.activation(out=c.lntmp[0:8, 0:1], in_=c.kc[:, 0:1], func=ACT.Ln,
                               bias=0.0, scale=1.0))
        e.wait_ge(c.ps.sem, c.ps.n)
        ins = e.activation(out=c.kc[:, 0:1], in_=c.lntmp[0:8, 0:1], func=ACT.Exp,
                           bias=0.0, scale=0.5)
        stab["s_kc"] = c.ps.inc(ins)

    if stab_lvl == 1:
        c.v_att_all = c.pv.n
        return
    dx_row = {}
    for ti in range(2, 2 + 99):
        break
    for ti in range(2 if stab_lvl >= 5 else 1):
        @block.tensor
        def _(e, ti=ti):
            e.wait_ge(c.ps.sem, stab["s_kc"])
            if ti == 1:
                e.wait_ge(c.pv.sem, stab[("v_rowv", 0)])
            e.matmul(c.psA[0:128, 512:513], c.selm[0:8, ti * 128:(ti + 1) * 128],
                     c.kc[0:8, 0:1], start=True, stop=True)
            ins = e.matmul(c.psA[:, 1024:1152], c.qn2[:, ti * 128:(ti + 1) * 128],
                           c.eye_f[:, :], start=True, stop=True)
            stab[("t_q", ti)] = c.pt.inc(ins)

        if stab_lvl == 2:
            continue
        sub = os.environ.get("ATH_STAB3", "abc")

        if "a" in sub:
            @block.vector
            def _(e, ti=ti):
                e.wait_ge(c.pt.sem, stab[("t_q", ti)])
                ins = e.tensor_copy(out=c.cper[:, 0:1], in_=c.psA[0:128, 512:513])
                stab[("v_cper", ti)] = c.pv.inc(ins)

        if "b" not in sub:
            continue

        @block.scalar
        def _(e, ti=ti):
            e.wait_ge(c.pt.sem, stab[("t_q", ti)])
            c.ps.inc(e.copy(out=c.sq2[:, :], in_=c.psA[:, 1024:1152]))
            e.wait_ge(c.ps.sem, c.ps.n)
            c.ps.inc(e.activation(out=c.sqt[:, :], in_=c.sq2[:, :], func=ACT.Ln,
                                  bias=0.0, scale=1.0))
            e.wait_ge(c.ps.sem, c.ps.n)
            ins = e.activation(out=c.sq2[:, :], in_=c.sqt[:, :], func=ACT.Exp,
                               bias=0.0, scale=0.5)
            stab[("s_sq", ti)] = c.ps.inc(ins)

        if "c" not in sub:
            continue

        @block.vector
        def _(e, ti=ti):
            e.wait_ge(c.ps.sem, stab[("s_sq", ti)])
            e.wait_ge(c.pv.sem, stab[("v_cper", ti)])
            if ti == 1:
                e.wait_ge(c.dx.sem, dx_row[0])
            ins = e.tensor_scalar(out=c.rowv[:, :], in0=c.sq2[:, :],
                                  scalar1=c.cper[:, 0:1], scalar2=-1.0,
                                  op0=OP.mult, op1=OP.mult)
            stab[("v_rowv", ti)] = c.pv.inc(ins)

        if stab_lvl == 3:
            continue

        @block.sync
        def _(e, ti=ti):
            e.wait_ge(c.pv.sem, stab[("v_rowv", ti)])
            e.wait_ge(c.dx.sem, c.dx.n)
            c.dx.inc(e.dma_start(out=c.q65[64:65, ti * 16384:(ti + 1) * 16384],
                                 in_=c.rowv[:, :]), 16)
            dx_row[ti] = c.dx.n

    if stab_lvl < 6:
        c.v_att_all = c.pv.n
        return
    if stab_lvl < 6:
        c.v_att_all = c.pv.n
        return
    t_s2 = {}
    s_exp = {}
    v_pm = {}
    t_pv = {}
    v_raw = {}
    v_dens = {}
    s_rden = {}
    t_bc = {}
    v_rbs = {}
    v_att = {}
    PREV_CC = [15, 13, 14]

    nheads_b = int(os.environ.get("ATH_BHEADS", "32"))

    def emit_tail(qh):
        @block.tensor
        def _(e):
            e.wait_ge(c.ps.sem, s_rden[qh])
            if (qh + 1, 3) in s_exp:
                e.wait_ge(c.ps.sem, s_exp[(qh + 1, 3)])
            e.matmul(s2[0][0:64, 0:512], c.ones1b[0:1, 0:64], c.rden[0:1, 0:512],
                     start=True, stop=True)
            ins = e.matmul(s2[0][0:64, 512:1024], c.ones1b[0:1, 0:64],
                           c.rden[0:1, 512:1024], start=True, stop=True)
            t_bc[qh] = c.pt.inc(ins)

        @block.vector
        def _(e):
            e.wait_ge(c.pt.sem, t_bc[qh])
            if (qh - 2) in v_att:
                e.wait_ge(c.pv.sem, v_att[qh - 2])
            rbsv = c.rbs2[qh % 2]
            ins = e.tensor_copy(out=rbsv, in_=s2[0][0:64, 0:1024])
            v_rbs[qh] = c.pv.inc(ins)
            e.wait_ge(c.pv.sem, v_rbs[qh])
            e.wait_ge(c.pv.sem, v_raw[qh])
            j = qh // 2
            r0 = (qh % 2) * 64
            dst = c.att_fm[r0:r0 + 64, j * 1024:(j + 1) * 1024]
            ins = e.tensor_tensor(out=dst[:, :], in0=dst[:, :],
                                  in1=rbsv, op=OP.mult)
            v_att[qh] = c.pv.inc(ins)

    for qh in range(nheads_b):
        kv = qh // 4
        pvb = pvb1

        def emit_pv(cc, qh=qh, kv=kv, pvb=pvb):
            p0 = cc // 2
            N = NQ_OF_C[cc]

            @block.tensor
            def _(e, qh=qh, cc=cc, p0=p0, N=N, kv=kv, pvb=pvb):
                e.wait_ge(c.pv.sem, v_pm[(qh, cc)])
                if cc == 0 and (qh - 1) in v_raw:
                    e.wait_ge(c.pv.sem, v_raw[qh - 1])
                lhs = c.v65[:, cc * 520 + kv * 65:cc * 520 + (kv + 1) * 65]
                pbf = pbuf[qh % 2]
                if p0 < 4:
                    wA = (4 - p0) * 128
                    e.matmul(pvb[0][:, p0 * 128:512], lhs,
                             pbf[:, OFFC[cc]:OFFC[cc] + wA],
                             start=(cc == 0), stop=(cc == 7), skip_group_check=True)
                    ins = e.matmul(pvb[1][:, 0:512], lhs,
                                   pbf[:, OFFC[cc] + wA:OFFC[cc] + N],
                                   start=(cc == 0), stop=(cc == 15), skip_group_check=True)
                else:
                    ins = e.matmul(pvb[1][:, (p0 - 4) * 128:512], lhs,
                                   pbf[:, OFFC[cc]:OFFC[cc] + N],
                                   start=False, stop=(cc == 15), skip_group_check=True)
                if cc == 15:
                    t_pv[qh] = c.pt.inc(ins)

        for cc in range(16):
            p0 = cc // 2
            N = NQ_OF_C[cc]
            si = cc % 3
            if cc == 4 and qh >= 1:
                emit_tail(qh - 1)

            @block.tensor
            def _(e, qh=qh, cc=cc, p0=p0, N=N, si=si, kv=kv):
                if qh == 0 and cc == 0:
                    e.wait_ge(c.dx.sem, dx_row[1])
                key = (qh, cc - 3)
                if key in s_exp:
                    e.wait_ge(c.ps.sem, s_exp[key])
                elif (qh - 1, PREV_CC[si]) in s_exp:
                    e.wait_ge(c.ps.sem, s_exp[(qh - 1, PREV_CC[si])])
                if cc == 6 and (qh - 1) in v_rbs:
                    e.wait_ge(c.pv.sem, v_rbs[qh - 1])
                if qh == 0 and cc < 3:
                    e.wait_ge(c.pv.sem, stab[("v_rowv", 1)])
                for half in range(2):
                    n = min(512, N - half * 512)
                    if n <= 0:
                        break
                    ins = e.matmul(s2[si][:, half * 512:half * 512 + n],
                                   c.k65[:, kv * 2048 + cc * 128:kv * 2048 + (cc + 1) * 128],
                                   c.q65[:, qh * 1024 + p0 * 128 + half * 512:
                                         qh * 1024 + p0 * 128 + half * 512 + n],
                                   start=True, stop=True)
                t_s2[(qh, cc)] = c.pt.inc(ins)

            @block.scalar
            def _(e, qh=qh, cc=cc, N=N, si=si):
                e.wait_ge(c.pt.sem, t_s2[(qh, cc)])
                if cc == 0 and (qh - 2) in t_pv:
                    e.wait_ge(c.pt.sem, t_pv[qh - 2])
                ins = e.activation(out=pbuf[qh % 2][:, OFFC[cc]:OFFC[cc] + N],
                                   in_=s2[si][:, 0:N], func=ACT.Exp, bias=0.0, scale=1.0)
                s_exp[(qh, cc)] = c.ps.inc(ins)

            @block.vector
            def _(e, qh=qh, cc=cc, si=si):
                e.wait_ge(c.ps.sem, s_exp[(qh, cc)])
                m = c.mA if cc % 2 == 0 else c.mB
                sl = pbuf[qh % 2][:, OFFC[cc]:OFFC[cc] + 128]
                ins = e.tensor_tensor(out=sl, in0=sl, in1=m[:, :], op=OP.mult)
                v_pm[(qh, cc)] = c.pv.inc(ins)

            if cc >= 2:
                emit_pv(cc - 2)
        emit_pv(14)
        emit_pv(15)

        @block.vector
        def _(e, qh=qh, pvb=pvb):
            e.wait_ge(c.pt.sem, t_pv[qh])
            if (qh - 1) in t_bc:
                e.wait_ge(c.pt.sem, t_bc[qh - 1])
            e.tensor_copy(out=c.dens[0:1, 0:512], in_=pvb[0][64:65, 0:512])
            ins = e.tensor_copy(out=c.dens[0:1, 512:1024], in_=pvb[1][64:65, 0:512])
            v_dens[qh] = c.pv.inc(ins)
            j = qh // 2
            r0 = (qh % 2) * 64
            dst = c.att_fm[r0:r0 + 64, j * 1024:(j + 1) * 1024]
            e.tensor_copy(out=dst[:, 0:512], in_=pvb[0][0:64, 0:512])
            ins = e.tensor_copy(out=dst[:, 512:1024], in_=pvb[1][0:64, 0:512])
            v_raw[qh] = c.pv.inc(ins)

        @block.scalar
        def _(e, qh=qh):
            e.wait_ge(c.pv.sem, v_dens[qh])
            c.ps.inc(e.activation(out=c.lntmp2[0:1, :], in_=c.dens[0:1, :], func=ACT.Ln,
                                  bias=0.0, scale=1.0))
            e.wait_ge(c.ps.sem, c.ps.n)
            ins = e.activation(out=c.rden[0:1, :], in_=c.lntmp2[0:1, :], func=ACT.Exp,
                               bias=0.0, scale=-1.0)
            s_rden[qh] = c.ps.inc(ins)

    emit_tail(nheads_b - 1)
    c.v_att_all = v_att[nheads_b - 1] if nheads_b else c.pv.n
    c.t_pv_last = t_pv[nheads_b - 1] if nheads_b else c.pt.n


def emit_dump_b(c):
    nc, block = c.nc, c.block

    @block.sync
    def _(e):
        e.wait_ge(c.pv.sem, c.v_att_all)
        c.dx.inc(e.dma_start(out=c.d_att, in_=c.att_fm[:, :]), 16)
        e.wait_ge(c.dx.sem, c.dx.n)


def emit_phase_c(c):
    nc, block = c.nc, c.block
    woh_buf = c.woh
    apf, aptok, xo, x2t, l3 = c.apf, c.aptok, c.xo, c.x2t, c.l3

    dwo = {}
    t_wo = {}
    v_apf = {}
    t_tr = {}
    v_apt = {}

    bar_pt = c.pt.n
    bar_ps = c.ps.n

    @block.gpsimd
    def _(e):
        e.wait_ge(c.pt.sem, c.t_pv_last)
        e.wait_ge(c.dw.sem, c.dw.n)
        ins = e.dma_start(out=woh_buf[:].rearrange("p (a b) -> p a b", a=16),
                          in_=c.wo_p[:, :, 0:1024])
        dwo[0] = c.dw.inc(ins, 16)
        e.wait_ge(c.dw.sem, dwo[0])
        ins = e.dma_start(out=c.woh2[:].rearrange("p (a b) -> p a b", a=16),
                          in_=c.wo_p[:, :, 1024:2048])
        dwo[1] = c.dw.inc(ins, 16)
        e.wait_ge(c.dw.sem, dwo[1])
        c.dw.inc(e.dma_start(out=c.gsc_s[:], in_=c.gsc), 16)
        c.dw.inc(e.dma_start(out=c.usc_s[:], in_=c.usc), 16)

    for h in range(2):
        for grp in range(2):
            for do in range(8):
                idx = (h * 2 + grp) * 8 + do

                @block.tensor
                def _(e, h=h, grp=grp, do=do, idx=idx):
                    if idx == 0:
                        e.wait_ge(c.dw.sem, dwo[0])
                        e.wait_ge(c.pv.sem, c.v_att_all)
                    if (h, grp, do) == (1, 0, 0):
                        e.wait_ge(c.dw.sem, dwo[1])
                    if idx - 2 >= 0:
                        e.wait_ge(c.pv.sem, v_apf[idx - 2])
                    wsrc = woh_buf if h == 0 else c.woh2
                    for dm in range(16):
                        ins = e.matmul(
                            c.bank(idx % 2),
                            wsrc[:, dm * 1024 + do * 128:dm * 1024 + (do + 1) * 128],
                            c.att_fm[:, dm * 1024 + grp * 512:dm * 1024 + (grp + 1) * 512],
                            start=(dm == 0), stop=(dm == 15))
                    t_wo[idx] = c.pt.inc(ins)

                @block.vector
                def _(e, idx=idx):
                    e.wait_ge(c.pt.sem, t_wo[idx])
                    if idx - 2 >= 0:
                        e.wait_ge(c.pt.sem, t_tr[idx - 2])
                    ins = e.tensor_copy(out=apf[idx % 2][:], in_=c.bank(idx % 2))
                    v_apf[idx] = c.pv.inc(ins)

                @block.tensor
                def _(e, idx=idx):
                    e.wait_ge(c.pv.sem, v_apf[idx])
                    if idx - 2 >= 0:
                        e.wait_ge(c.ps.sem, v_apt[idx - 2])
                    for q in range(4):
                        ins = e.matmul(c.bank(2 + idx % 2)[:, q * 128:(q + 1) * 128],
                                       apf[idx % 2][:, q * 128:(q + 1) * 128],
                                       c.eye_f[:, :], start=True, stop=True)
                    t_tr[idx] = c.pt.inc(ins)

                @block.scalar
                def _(e, h=h, grp=grp, do=do, idx=idx):
                    e.wait_ge(c.pt.sem, t_tr[idx])
                    av = aptok[:].rearrange("p (t d) -> p t d", t=8)
                    fcol = h * 1024 + do * 128
                    ins = e.copy(
                        out=av[:, grp * 4:(grp + 1) * 4, fcol:fcol + 128],
                        in_=c.bank(2 + idx % 2).rearrange("p (q d) -> p q d", q=4))
                    v_apt[idx] = c.ps.inc(ins)

    s_r2 = {}
    v_x2 = {}
    s_r3 = {}
    v_l3 = {}
    t_l3t = {}
    v_l3fm = {}
    dx_xo = {}
    dx_sp = {}

    for p in range(8):
        @block.sync
        def _(e, p=p):
            if p == 0:
                e.wait_ge(c.pv.sem, c.v_att_all)
                e.wait_ge(c.pt.sem, bar_pt)
                e.wait_ge(c.ps.sem, bar_ps)
            else:
                e.wait_ge(c.pv.sem, v_x2[p - 1])
            e.wait_ge(c.dx.sem, c.dx.n)
            c.dx.inc(e.dma_start(out=xo[p % 2][:], in_=c.xown[p * 128:(p + 1) * 128, :]), 16)
            dx_xo[p] = c.dx.n

        @block.vector
        def _(e, p=p):
            need = max(v_apt[(h2 * 2 + p // 4) * 8 + do2]
                       for h2 in range(2) for do2 in range(8))
            e.wait_ge(c.ps.sem, need)
            av = aptok[:].rearrange("p (t d) -> p t d", t=8)
            c.pv.inc(_ln_stats_edge(c, e, av[:, p, :], 2048))

        @block.scalar
        def _(e, p=p):
            e.wait_ge(c.pv.sem, c.pv.n)
            s_r2[p] = c.ps.inc(_rstd_emit(c, e))

        @block.vector
        def _(e, p=p):
            e.wait_ge(c.ps.sem, s_r2[p])
            e.wait_ge(c.dx.sem, dx_xo[p])
            if p >= 1:
                e.wait_ge(c.pv.sem, v_l3[p - 1])
            av = aptok[:].rearrange("p (t d) -> p t d", t=8)
            c.pv.inc(e.tensor_scalar(out=x2t[p % 2][:], in0=av[:, p, :],
                                     scalar1=c.mv[:, 0:1], scalar2=c.rstd[:, 0:1],
                                     op0=OP.subtract, op1=OP.mult))
            e.wait_ge(c.pv.sem, c.pv.n)
            ins = e.tensor_add(x2t[p % 2][:], x2t[p % 2][:], xo[p % 2][:])
            v_x2[p] = c.pv.inc(ins)
            e.wait_ge(c.pv.sem, v_x2[p])
            c.pv.inc(_ln_stats_edge(c, e, x2t[p % 2][:], 2048))

        @block.sync
        def _(e, p=p):
            e.wait_ge(c.pv.sem, v_x2[p])
            e.wait_ge(c.dx.sem, c.dx.n)
            c.dx.inc(e.dma_start(out=c.x2scr[p * 128:(p + 1) * 128, :], in_=x2t[p % 2][:]), 16)
            dx_sp[p] = c.dx.n

        @block.scalar
        def _(e, p=p):
            e.wait_ge(c.pv.sem, c.pv.n)
            s_r3[p] = c.ps.inc(_rstd_emit(c, e))

        @block.vector
        def _(e, p=p):
            e.wait_ge(c.ps.sem, s_r3[p])
            ins = e.tensor_scalar(out=l3[p % 2][:], in0=x2t[p % 2][:],
                                  scalar1=c.mv[:, 0:1], scalar2=c.rstd[:, 0:1],
                                  op0=OP.subtract, op1=OP.mult)
            v_l3[p] = c.pv.inc(ins)

        @block.tensor
        def _(e, p=p):
            e.wait_ge(c.pv.sem, v_l3[p])
            if p >= 2:
                e.wait_ge(c.ps.sem, v_l3fm[p - 2])
            base = (p % 2) * 4
            for dm in range(16):
                ins = e.matmul(c.bank(base + dm // 4)[:, (dm % 4) * 128:(dm % 4 + 1) * 128],
                               l3[p % 2][:, dm * 128:(dm + 1) * 128], c.eye_b[:, :],
                               start=True, stop=True)
            t_l3t[p] = c.pt.inc(ins)

        @block.scalar
        def _(e, p=p):
            e.wait_ge(c.pt.sem, t_l3t[p])
            base = (p % 2) * 4
            lv = c.ln3_fm[:].rearrange("p (d o) -> p d o", d=16)
            for bg in range(4):
                ins = e.copy(
                    out=lv[:, bg * 4:(bg + 1) * 4, p * 128:(p + 1) * 128],
                    in_=c.bank(base + bg).rearrange("p (q d) -> p q d", q=4))
            v_l3fm[p] = c.ps.inc(ins)

    c.v_l3fm_all = v_l3fm[7]
    c.dx_sp = dx_sp

    @block.gpsimd
    def _(e):
        e.wait_ge(c.pt.sem, t_wo[31])
        e.wait_ge(c.dw.sem, c.dw.n)
        c.dw.inc(e.dma_start(out=c.wgt[0][:], in_=c.wg_p[0]), 16)
        e.wait_ge(c.dw.sem, c.dw.n)
        c.dw.inc(e.dma_start(out=c.wut[0][:], in_=c.wu_p[0]), 16)
        c.dw_pref = c.dw.n


def emit_dump_c(c):
    nc, block = c.nc, c.block

    @block.sync
    def _(e):
        e.wait_ge(c.ps.sem, c.v_l3fm_all)
        c.dx.inc(e.dma_start(out=c.d_ln3, in_=c.ln3_fm[:, :]), 16)
        for p in range(8):
            e.wait_ge(c.dx.sem, c.dx_sp[p])
        for p in range(8):
            c.dx.inc(e.dma_start(out=c.d_x2[p * 128:(p + 1) * 128, :],
                                 in_=c.x2scr[p * 128:(p + 1) * 128, :]), 16)
        e.wait_ge(c.dx.sem, c.dx.n)


def emit_phase_d(c):
    nc, block = c.nc, c.block
    wgt, wut, gtmp, gs, us, s3, sdn = c.wgt, c.wut, c.gtmp, c.gs, c.us, c.s3, c.sdn

    bar_pt = c.pt.n
    bar_ps = c.ps.n

    @block.scalar
    def _(e):
        _preload_set(c, e, [ACT.Silu])

    @block.tensor
    def _(e):
        e.wait_ge(c.ps.sem, c.v_l3fm_all)
        for hc in range(16):
            for half in range(2):
                ins = e.matmul(c.psA[0:1, half * 512:(half + 1) * 512],
                               c.onescb[:, 0:1],
                               c.ln3_fm[:, hc * 1024 + half * 512:hc * 1024 + (half + 1) * 512],
                               start=(hc == 0), stop=(hc == 15))
        c.pt.inc(ins)

    @block.vector
    def _(e):
        e.wait_ge(c.pt.sem, c.pt.n)
        e.wait_ge(c.dx.sem, c.dx.n)
        c.pv.inc(e.tensor_copy(out=s3[:], in_=c.psA[0:1, 0:1024]))
    v_s3 = c.pv.n

    @block.tensor
    def _(e):
        e.wait_ge(c.pv.sem, v_s3)
        for half in range(2):
            ins = e.matmul(c.psB[:, half * 512:(half + 1) * 512], c.ones1f[0:1, :],
                           s3[0:1, half * 512:(half + 1) * 512], start=True, stop=True)
        c.pt.inc(ins)

    @block.vector
    def _(e):
        e.wait_ge(c.pt.sem, c.pt.n)
        ins = e.tensor_scalar(out=c.cs3[:, :], in0=c.psB[:, 0:1024], scalar1=128.0,
                              scalar2=None, op0=OP.mult)
        c.pv.inc(ins)
    v_cs3 = c.pv.n

    dw_g = {}
    t_gu = {}
    s_ge = {}
    v_ue = {}
    v_h = {}

    dw_g[0] = c.dw_pref
    for ib in range(64):
        if ib >= 1:
            @block.gpsimd
            def _(e, ib=ib):
                if ib >= 2:
                    e.wait_ge(c.pt.sem, t_gu[ib - 2])
                e.wait_ge(c.dw.sem, c.dw.n)
                c.dw.inc(e.dma_start(out=wgt[ib % 2][:], in_=c.wg_p[ib]), 16)
                e.wait_ge(c.dw.sem, c.dw.n)
                ins = e.dma_start(out=wut[ib % 2][:], in_=c.wu_p[ib])
                dw_g[ib] = c.dw.inc(ins, 16)

        @block.tensor
        def _(e, ib=ib):
            e.wait_ge(c.dw.sem, dw_g[ib])
            if ib == 0:
                e.wait_ge(c.pv.sem, v_cs3)
            if ib >= 2:
                e.wait_ge(c.ps.sem, s_ge[ib - 2])
                e.wait_ge(c.pv.sem, v_ue[ib - 2])
            gb = ib % 2
            gbank = c.psA[:, gb * 1024:(gb + 1) * 1024]
            ubank = c.psB[:, gb * 1024:(gb + 1) * 1024]
            for half in range(2):
                for hc in range(16):
                    e.matmul(gbank[:, half * 512:(half + 1) * 512],
                             wgt[ib % 2][:, hc * 128:(hc + 1) * 128],
                             c.ln3_fm[:, hc * 1024 + half * 512:hc * 1024 + (half + 1) * 512],
                             start=(hc == 0), stop=(hc == 15))
                for hc in range(16):
                    ins = e.matmul(ubank[:, half * 512:(half + 1) * 512],
                                   wut[ib % 2][:, hc * 128:(hc + 1) * 128],
                                   c.ln3_fm[:, hc * 1024 + half * 512:hc * 1024 + (half + 1) * 512],
                                   start=(hc == 0), stop=(hc == 15))
            t_gu[ib] = c.pt.inc(ins)

        @block.vector
        def _(e, ib=ib):
            e.wait_ge(c.pt.sem, t_gu[ib])
            if ib >= 1:
                e.wait_ge(c.ps.sem, s_ge[ib - 1])
                e.wait_ge(c.pv.sem, v_h[ib - 1])
            gbank = c.psA[:, (ib % 2) * 1024:(ib % 2 + 1) * 1024]
            ubank = c.psB[:, (ib % 2) * 1024:(ib % 2 + 1) * 1024]
            ins = e.tensor_tensor(out=gtmp[:], in0=gbank, in1=c.cs3[:, :], op=OP.subtract)
            v_gsub = c.pv.inc(ins)
            ins = e.tensor_tensor(out=c.ubf[:], in0=ubank, in1=c.cs3[:, :], op=OP.subtract)
            v_ue[ib] = c.pv.inc(ins)

        @block.scalar
        def _(e, ib=ib):
            e.wait_ge(c.pv.sem, v_ue[ib])
            ins = e.activation(out=gs[:], in_=gtmp[:], func=ACT.Silu,
                               bias=0.0, scale=c.gsc_s[:, ib:ib + 1])
            s_ge[ib] = c.ps.inc(ins)

        @block.vector
        def _(e, ib=ib):
            e.wait_ge(c.ps.sem, s_ge[ib])
            ins = e.scalar_tensor_tensor(out=c.hdn[:, ib * 1024:(ib + 1) * 1024],
                                         in0=gs[:], scalar=c.usc_s[:, ib:ib + 1],
                                         in1=c.ubf[:], op0=OP.mult, op1=OP.mult)
            v_h[ib] = c.pv.inc(ins)

    @block.tensor
    def _(e):
        e.wait_ge(c.pv.sem, v_h[63])
        for ic in range(64):
            for half in range(2):
                ins = e.matmul(c.psA[0:1, half * 512:(half + 1) * 512],
                               c.onescb[:, 0:1],
                               c.hdn[:, ic * 1024 + half * 512:ic * 1024 + (half + 1) * 512],
                               start=(ic == 0), stop=(ic == 63))
        c.pt.inc(ins)

    @block.vector
    def _(e):
        e.wait_ge(c.pt.sem, c.pt.n)
        c.pv.inc(e.tensor_copy(out=sdn[:], in_=c.psA[0:1, 0:1024]))
    v_sdn = c.pv.n

    bar_d2_pt = c.pt.n
    wdt = c.wdt
    dw_d = {}
    t_dn = {}
    v_y = {}

    @block.sync
    def _(e):
        e.wait_ge(c.pt.sem, bar_d2_pt)
        e.wait_ge(c.ps.sem, c.ps.n)
        e.wait_ge(c.dx.sem, c.dx.n)
        c.dx.inc(e.dma_start(out=c.dsc_s[:], in_=c.dsc), 16)
    dx_dsc = c.dx.n

    @block.tensor
    def _(e):
        e.wait_ge(c.dx.sem, dx_dsc)
        e.wait_ge(c.pv.sem, v_sdn)
        for j in range(4):
            ins = e.matmul(c.bank(j), c.ones1f[0:1, :],
                           c.dsc_s[0:1, j * 512:(j + 1) * 512], start=True, stop=True)
        c.pt.inc(ins)

    @block.vector
    def _(e):
        e.wait_ge(c.pt.sem, c.pt.n)
        e.wait_ge(c.pv.sem, v_h[63])
        e.memset(c.m128f[:], -128.0)
        c.pv.inc(e.tensor_copy(out=c.dsc_bc[:], in_=c.psA[:, :]))
    v_dscbc = c.pv.n

    for hq in range(4):
        for icg in range(16):
            gi = hq * 16 + icg

            @block.gpsimd
            def _(e, hq=hq, icg=icg, gi=gi):
                if gi < 3:
                    e.wait_ge(c.pt.sem, bar_d2_pt)
                else:
                    e.wait_ge(c.pt.sem, t_dn[gi - 3])
                e.wait_ge(c.dw.sem, c.dw.n)
                ins = e.dma_start(out=wdt[gi % 3][:], in_=c.wd_p[hq, icg])
                dw_d[gi] = c.dw.inc(ins, 16)

            @block.tensor
            def _(e, hq=hq, icg=icg, gi=gi):
                e.wait_ge(c.dw.sem, dw_d[gi])
                if icg == 0:
                    if gi == 0:
                        e.wait_ge(c.pv.sem, v_dscbc)
                    else:
                        e.wait_ge(c.pv.sem, v_y[hq - 1])
                for ici in range(4):
                    ic = icg * 4 + ici
                    for tt in range(8):
                        ins = e.matmul(c.bank(tt),
                                       c.hdn[:, ic * 1024 + tt * 128:ic * 1024 + (tt + 1) * 128],
                                       wdt[gi % 3][:, ici * 512:(ici + 1) * 512],
                                       start=(ic == 0), stop=False, skip_group_check=True)
                if icg == 15:
                    for tt in range(8):
                        ins = e.matmul(c.bank(tt),
                                       sdn[0:1, tt * 128:(tt + 1) * 128],
                                       c.m128f[0:1, 0:512],
                                       start=False, stop=True, skip_group_check=True)
                t_dn[gi] = c.pt.inc(ins)

        @block.vector
        def _(e, hq=hq):
            e.wait_ge(c.pt.sem, t_dn[hq * 16 + 15])
            yv = c.y[:].rearrange("p (t d) -> p t d", t=8)
            for tt in range(8):
                ins = e.tensor_tensor(out=yv[:, tt, hq * 512:(hq + 1) * 512],
                                      in0=c.bank(tt),
                                      in1=c.dsc_bc[:, hq * 512:(hq + 1) * 512], op=OP.mult)
            v_y[hq] = c.pv.inc(ins)

    c.v_y_all = v_y[3]


def emit_phase_e(c):
    nc, block = c.nc, c.block
    x2r = c.x2r

    @block.scalar
    def _(e):
        _preload_set(c, e, [ACT.Exp, ACT.Ln])

    dx_r = {}
    s_r4 = {}
    v_o = {}

    for p in range(8):
        @block.sync
        def _(e, p=p):
            if p == 0:
                e.wait_ge(c.pv.sem, c.v_y_all)
                for pp in range(8):
                    e.wait_ge(c.dx.sem, c.dx_sp[pp])
            if p >= 2:
                e.wait_ge(c.pv.sem, v_o[p - 2])
            e.wait_ge(c.dx.sem, c.dx.n)
            c.dx.inc(e.dma_start(out=x2r[p % 2][:], in_=c.x2scr[p * 128:(p + 1) * 128, :]), 16)
            dx_r[p] = c.dx.n

        @block.vector
        def _(e, p=p):
            e.wait_ge(c.pv.sem, c.v_y_all)
            yv = c.y[:].rearrange("p (t d) -> p t d", t=8)
            c.pv.inc(_ln_stats_edge(c, e, yv[:, p, :], 2048))

        @block.scalar
        def _(e, p=p):
            e.wait_ge(c.pv.sem, c.pv.n)
            s_r4[p] = c.ps.inc(_rstd_emit(c, e))

        @block.vector
        def _(e, p=p):
            e.wait_ge(c.ps.sem, s_r4[p])
            e.wait_ge(c.dx.sem, dx_r[p])
            yv = c.y[:].rearrange("p (t d) -> p t d", t=8)
            for half in range(2):
                c.pv.inc(e.tensor_scalar(
                    out=c.ln4b[:], in0=yv[:, p, half * 1024:(half + 1) * 1024],
                    scalar1=c.mv[:, 0:1], scalar2=c.rstd[:, 0:1],
                    op0=OP.subtract, op1=OP.mult))
                e.wait_ge(c.pv.sem, c.pv.n)
                sl = x2r[p % 2][:, half * 1024:(half + 1) * 1024]
                v_o[p] = c.pv.inc(e.tensor_add(sl, sl, c.ln4b[:]))
                if half == 0:
                    e.wait_ge(c.pv.sem, v_o[p])

        @block.sync
        def _(e, p=p):
            e.wait_ge(c.pv.sem, v_o[p])
            e.wait_ge(c.dx.sem, c.dx.n)
            c.dx.inc(e.dma_start(out=c.out[p * 128:(p + 1) * 128, :], in_=x2r[p % 2][:]), 16)

    @block.sync
    def _(e):
        e.wait_ge(c.dx.sem, c.dx.n)


def make_in_maps(x, wq, wk, wv, wo, gw, gscale, uw, uscale, dw, dscale):
    common = {}
    common["wq_p"] = np.ascontiguousarray(wq.reshape(16, 128, 2048).transpose(1, 0, 2))
    common["wk_p"] = np.ascontiguousarray(wk.reshape(16, 128, 512).transpose(1, 0, 2))
    common["wv_p"] = np.ascontiguousarray(wv.reshape(16, 128, 512).transpose(1, 0, 2))
    common["wo_p"] = np.ascontiguousarray(wo.reshape(16, 128, 2048).transpose(1, 0, 2))
    wgT = np.ascontiguousarray(gw.T)
    wuT = np.ascontiguousarray(uw.T)
    common["wg_p"] = np.ascontiguousarray(
        wgT.reshape(16, 128, 64, 128).transpose(2, 1, 0, 3).reshape(64, 128, 2048))
    common["wu_p"] = np.ascontiguousarray(
        wuT.reshape(16, 128, 64, 128).transpose(2, 1, 0, 3).reshape(64, 128, 2048))
    wdT = np.ascontiguousarray(dw.T)
    common["wd_p"] = np.ascontiguousarray(
        wdT.reshape(16, 4, 128, 4, 512).transpose(3, 0, 2, 1, 4).reshape(4, 16, 128, 2048))
    common["gsc"] = np.ascontiguousarray(gscale.reshape(64, 128).T)
    common["usc"] = np.ascontiguousarray(uscale.reshape(64, 128).T)
    common["dsc"] = np.ascontiguousarray(dscale.reshape(1, 2048))
    common["eye"] = np.eye(128, dtype=np.float32)
    colh = np.arange(256) // 8
    sel = (np.arange(8)[:, None] == (colh[None, :] // 4)).astype(np.float32)
    common["selmap"] = sel

    import jax
    import jax.numpy as jnp
    with jax.default_device(jax.devices("cpu")[0]):
        inv_freq = 1.0 / (10000.0 ** (jnp.arange(0, HD, 2, dtype=jnp.float32) / HD))
        pos_j = jnp.arange(SEQ, dtype=jnp.float32)
        freqs = pos_j[:, None] * inv_freq[None, :]
        cosf = np.asarray(jnp.cos(freqs)).astype(np.float32)
        sinf = np.asarray(jnp.sin(freqs)).astype(np.float32)

    rr = np.arange(128)
    in_maps = []
    for core in range(8):
        b, delta = core // 2, core % 2
        m = dict(common)
        perm = np.where(rr < 64, rr + 64, rr - 64) if delta else rr.copy()
        gperm = (np.arange(16)[:, None] * 128 + perm[None, :]).ravel()
        m["xb"] = np.ascontiguousarray(x[b][gperm])
        ownrows = (np.arange(16)[:, None] * 128 + delta * 64 + np.arange(64)[None, :]).ravel()
        m["xown"] = np.ascontiguousarray(x[b][ownrows])
        pp = np.arange(8)[None, :]
        posq = (2 * pp + (rr[:, None] >= 64)) * 128 + delta * 64 + (rr[:, None] % 64)
        m["qcos"] = np.ascontiguousarray((cosf[posq] / 8.0).astype(np.float32).reshape(128, 256))
        m["qsin"] = np.ascontiguousarray((sinf[posq] / 8.0).astype(np.float32).reshape(128, 256))
        post = np.arange(16)[None, :] * 128 + perm[:, None]
        m["kcos"] = np.ascontiguousarray(cosf[post].astype(np.float32).reshape(128, 512))
        m["ksin"] = np.ascontiguousarray(sinf[post].astype(np.float32).reshape(128, 512))
        kk = perm[:, None].astype(np.int64)
        col = np.arange(128)[None, :]
        qrel = delta * 64 + (col % 64)
        mA = np.ones((128, 128), np.float32)
        mA[:, 0:64] = np.where(kk > qrel[:, 0:64], np.float32(0.0), np.float32(1.0))
        mB = np.zeros((128, 128), np.float32)
        mB[:, 64:128] = np.where(kk > qrel[:, 64:128], np.float32(0.0), np.float32(1.0))
        m["maskA"] = mA.astype(ml_dtypes.bfloat16)
        m["maskB"] = mB.astype(ml_dtypes.bfloat16)
        in_maps.append(m)
    return in_maps


def kernel(L0_attn_x, L0_attn_wq, L0_attn_wk, L0_attn_wv, L0_attn_wo,
           L0_ffn_gate_w, L0_ffn_gate_scale, L0_ffn_up_w, L0_ffn_up_scale,
           L0_ffn_down_w, L0_ffn_down_scale):
    x = np.asarray(L0_attn_x, dtype=np.float32)
    in_maps = make_in_maps(x,
                           np.asarray(L0_attn_wq, np.float32),
                           np.asarray(L0_attn_wk, np.float32),
                           np.asarray(L0_attn_wv, np.float32),
                           np.asarray(L0_attn_wo, np.float32),
                           np.asarray(L0_ffn_gate_w, np.int32),
                           np.asarray(L0_ffn_gate_scale, np.float32),
                           np.asarray(L0_ffn_up_w, np.int32),
                           np.asarray(L0_ffn_up_scale, np.float32),
                           np.asarray(L0_ffn_down_w, np.int32),
                           np.asarray(L0_ffn_down_scale, np.float32))
    nc = build_nc("full")
    res = run_bass_kernel_spmd(nc, in_maps, core_ids=list(range(8)))
    out = np.empty((4, SEQ, DIM), dtype=np.float32)
    for core in range(8):
        b, delta = core // 2, core % 2
        ownrows = (np.arange(16)[:, None] * 128 + delta * 64 + np.arange(64)[None, :]).ravel()
        out[b][ownrows] = res.results[core]["out"]
    return out
